# version 01
# baseline (speedup 1.0000x reference)
import numpy as np
from contextlib import ExitStack

import concourse.bass as bass
import concourse.mybir as mybir

F32 = mybir.dt.float32
BF16 = mybir.dt.bfloat16
ALU = mybir.AluOpType
AF = mybir.ActivationFunctionType
AX = mybir.AxisListType

D = 2048
NCH = 16
T_OWN = 2048
T_ALL = 4096
SCALE_L0 = float((128 + 64) ** -0.5)
FSC1 = 1.0 / (64.0 * float(np.sqrt(2048.0)))
FSC2 = 1.0 / (64.0 * float(np.sqrt(8192.0)))
PI = float(np.pi)
SWAP_MASK = [m for i in range(0, 32, 2) for m in (i + 1, i)]


class Sem:
    def __init__(self, h):
        self.h = h
        self.n = 0

    def inc(self, inst, k=1):
        self.n += k
        inst.then_inc(self.h, k)
        return self.n


class KB:
    def __init__(self):
        self.nc = bass.Bass("TRN2", target_bir_lowering=False, debug=False,
                            detect_race_conditions=False)
        self.es = ExitStack()
        self.block = None
        self._sems = {}

    def sem(self, name):
        if name not in self._sems:
            self._sems[name] = Sem(self.es.enter_context(self.nc.semaphore(name)))
        return self._sems[name]

    def sb(self, name, shape, dtype, stack=None):
        return (stack or self.es).enter_context(self.nc.sbuf_tensor(name, shape, dtype))

    def ps(self, name, shape, dtype=F32, stack=None):
        return (stack or self.es).enter_context(self.nc.psum_tensor(name, shape, dtype))

    def din(self, name, shape, dtype=F32):
        return self.nc.dram_tensor(name, shape, dtype, kind="ExternalInput").ap()

    def dout(self, name, shape, dtype=F32):
        return self.nc.dram_tensor(name, shape, dtype, kind="ExternalOutput").ap()

    def dscr(self, name, shape, dtype=F32, dbg=False):
        return self.nc.dram_tensor(name, shape, dtype,
                                   kind="ExternalOutput" if dbg else "Internal").ap()

    def v(self, f): self.block.vector(f)
    def s(self, f): self.block.scalar(f)
    def t(self, f): self.block.tensor(f)
    def g(self, f): self.block.gpsimd(f)
    def sy(self, f): self.block.sync(f)


class Spine:

    def __init__(self, kb):
        self.kb = kb
        self.chains = {}

    def chain(self, name):
        if name not in self.chains:
            self.chains[name] = self.kb.sem("ch_" + name)
        return self.chains[name]

    def step(self, chain, emitfn, body, extra=()):
        c = self.chain(chain)
        cur = c.n
        def f(eng):
            if cur > 0:
                eng.wait_ge(c.h, cur)
            for (sem, val) in extra:
                if val > 0:
                    eng.wait_ge(sem.h, val)
            ins = body(eng)
            assert ins is not None, "step body must return its last instruction"
            c.inc(ins)
        emitfn(f)
        return c.n

    def T(self, body, extra=(), chain='T'): return self.step(chain, self.kb.t, body, extra)
    def V(self, body, extra=(), chain='E'): return self.step(chain, self.kb.v, body, extra)
    def S(self, body, extra=(), chain='E'): return self.step(chain, self.kb.s, body, extra)
    def G(self, body, extra=(), chain='E'): return self.step(chain, self.kb.g, body, extra)

    def dma(self, queue_emit, dsem, body, extra=()):
        def f(eng):
            for (sem, val) in extra:
                if val > 0:
                    eng.wait_ge(sem.h, val)
            for ins in body(eng):
                dsem.inc(ins, 16)
        queue_emit(f)
        return dsem.n


def emit_ln(kb, tag, xin_r, t_tiles, xn_dst, cm, stack, pre_xn=None, post_tile=None,
            gate=()):
    xs = kb.sb("ln_xs" + tag, [128, NCH * 512], F32, stack)
    xb = kb.sb("ln_xb" + tag, [128, NCH * 512], BF16, stack)
    rows = kb.sb("ln_rows" + tag, [1, 2048], F32, stack)
    tmp = kb.sb("ln_tmp" + tag, [128, 512], F32, stack)
    ps_stat = kb.ps("ln_ps_st" + tag, [1, 1024], F32, stack)
    ps_b0 = kb.ps("ln_ps_b0" + tag, [128, 512], F32, stack)
    ps_b1 = kb.ps("ln_ps_b1" + tag, [128, 512], F32, stack)
    s_ld = kb.sem("lnld" + tag)
    s_ldb = kb.sem("lnldb" + tag)
    s_sq = kb.sem("lnsq" + tag)
    s_mm = kb.sem("lnmm" + tag)
    s_row = kb.sem("lnrow" + tag)
    s_out = kb.sem("lnout" + tag)

    for i, t0 in enumerate(t_tiles):
        def loads(sync, i=i, t0=t0):
            for (sem, val) in gate:
                if val > 0:
                    sync.wait_ge(sem.h, val)
            if i >= 1:
                sync.wait_ge(s_out.h, i)
            s_ld.inc(sync.dma_start(
                out=xs[:, :].rearrange("p (c t) -> p c t", c=NCH),
                in_=xin_r[:, :, t0:t0 + 512]), 16)
        kb.sy(loads)

        def loadsb(g, i=i, t0=t0):
            for (sem, val) in gate:
                if val > 0:
                    g.wait_ge(sem.h, val)
            if i >= 1:
                g.wait_ge(s_out.h, i)
            s_ldb.inc(g.dma_start(
                out=xb[:, :].rearrange("p (c t) -> p c t", c=NCH),
                in_=xin_r[:, :, t0:t0 + 512]), 16)
        kb.g(loadsb)

        def stats1(t, i=i):
            t.wait_ge(s_ldb.h, 16 * (i + 1))
            ins = None
            for c in range(NCH):
                ins = t.matmul(ps_stat[0:1, 0:512], cm['onesb'][:, :],
                               xb[:, c * 512:(c + 1) * 512],
                               start=(c == 0), stop=(c == NCH - 1))
            s_mm.inc(ins)
        kb.t(stats1)

        def squares(s, i=i):
            s.wait_ge(s_mm.h, s_mm.n)
            ins = None
            for c in range(NCH):
                ins = s.activation(out=xb[:, c * 512:(c + 1) * 512],
                                   in_=xb[:, c * 512:(c + 1) * 512], func=AF.Square)
            s_sq.inc(ins)
        kb.s(squares)

        def stats2(t, i=i):
            t.wait_ge(s_sq.h, s_sq.n)
            ins = None
            for c in range(NCH):
                ins = t.matmul(ps_stat[0:1, 512:1024], cm['onesb'][:, :],
                               xb[:, c * 512:(c + 1) * 512],
                               start=(c == 0), stop=(c == NCH - 1))
            s_mm.inc(ins)
        kb.t(stats2)

        def rowmath(v, i=i):
            v.wait_ge(s_mm.h, s_mm.n)
            v.tensor_scalar(out=rows[0:1, 0:512], in0=ps_stat[0:1, 0:512],
                            scalar1=1.0 / 2048, scalar2=None, op0=ALU.mult)
            v.tensor_scalar(out=rows[0:1, 512:1024], in0=ps_stat[0:1, 512:1024],
                            scalar1=1.0 / 2048, scalar2=None, op0=ALU.mult)
            v.tensor_tensor(out=rows[0:1, 1024:1536], in0=rows[0:1, 0:512],
                            in1=rows[0:1, 0:512], op=ALU.mult)
            v.tensor_tensor(out=rows[0:1, 1024:1536], in0=rows[0:1, 512:1024],
                            in1=rows[0:1, 1024:1536], op=ALU.subtract)
            ins = v.tensor_scalar(out=rows[0:1, 1024:1536], in0=rows[0:1, 1024:1536],
                                  scalar1=1e-5, scalar2=None, op0=ALU.add)
            s_row.inc(ins)
        kb.v(rowmath)

        def rowsqrt(s, i=i):
            s.wait_ge(s_row.h, s_row.n)
            ins = s.activation(out=rows[0:1, 1536:2048], in_=rows[0:1, 1024:1536],
                               func=AF.Sqrt)
            s_row.inc(ins)
        kb.s(rowsqrt)

        def rowfin(v, i=i):
            v.wait_ge(s_row.h, s_row.n)
            v.reciprocal(out=rows[0:1, 1024:1536], in_=rows[0:1, 1536:2048])
            ins = v.tensor_tensor(out=rows[0:1, 1536:2048], in0=rows[0:1, 0:512],
                                  in1=rows[0:1, 1024:1536], op=ALU.mult)
            s_row.inc(ins)
        kb.v(rowfin)

        def bcasts(t, i=i):
            t.wait_ge(s_row.h, s_row.n)
            t.matmul(ps_b0[:, :], cm['onesrow'][0:1, :], rows[0:1, 1024:1536],
                     start=True, stop=True)
            ins = t.matmul(ps_b1[:, :], cm['onesrow'][0:1, :], rows[0:1, 1536:2048],
                           start=True, stop=True)
            s_mm.inc(ins)
        kb.t(bcasts)

        def xns(v, i=i, t0=t0):
            v.wait_ge(s_mm.h, s_mm.n)
            v.wait_ge(s_ld.h, 16 * (i + 1))
            if pre_xn is not None:
                pre_xn(v, i)
            ins = None
            for c in range(NCH):
                v.tensor_tensor(out=tmp[:, :], in0=xs[:, c * 512:(c + 1) * 512],
                                in1=ps_b0[:, :], op=ALU.mult)
                ins = v.tensor_tensor(out=xn_dst(c, t0), in0=tmp[:, :],
                                      in1=ps_b1[:, :], op=ALU.subtract)
            s_out.inc(ins)
        kb.v(xns)
        if post_tile is not None:
            post_tile(i, t0)
    return s_out


def build_l0(dbg=False):
    kb = KB()
    nc = kb.nc
    xin = kb.din("xin", [D, T_ALL])
    wq_n = kb.din("wq_n", [D, 2048])
    wq_r = kb.din("wq_r", [D, 1024])
    wkva_c = kb.din("wkva_c", [D, 512])
    wkva_r = kb.din("wkva_r", [D, 64])
    kvg = kb.din("kvg", [128, 4])
    kvbt = kb.din("kvbt", [128, 4])
    wkvb_k = kb.din("wkvb_k", [512, 2048])
    wkvb_v = kb.din("wkvb_v", [512, 2048])
    wout = kb.din("wout", [D, D])
    freqsT = kb.din("freqsT", [32, 2048])
    su_ug = kb.din("su_ug", [128, 16])
    su_dp = kb.din("su_dp", [128, 64])
    wsc_ug = kb.din("wsc_ug", [128, 128])
    wsc_dp = kb.din("wsc_dp", [128, 16])
    wug = kb.din("wug", [D, 16384])
    wdownT = kb.din("wdownT", [8192, D])
    c_onesb = kb.din("c_onesb", [128, 1], BF16)
    c_onesf = kb.din("c_onesf", [128, 1])
    c_onesro = kb.din("c_onesro", [1, 128])
    c_h128 = kb.din("c_h128", [128, 128], BF16)
    c_dupc = kb.din("c_dupc", [32, 128])
    c_dups = kb.din("c_dups", [32, 128])
    x1h = kb.dout("x1h", [D, T_OWN])

    kscr_n = kb.dscr("kscr_n", [2048, T_ALL], BF16, dbg)
    kscr_r = kb.dscr("kscr_r", [64, T_ALL], BF16, dbg)
    vscr = kb.dscr("vscr", [T_ALL, 2048], BF16, dbg)
    qscr_n = kb.dscr("qscr_n", [2048, T_OWN], BF16, dbg)
    qscr_r = kb.dscr("qscr_r", [1024, T_OWN], BF16, dbg)
    attnscr = kb.dscr("attnscr", [2048, T_OWN], BF16, dbg)
    s0scr = kb.dscr("s0scr", [D, T_OWN], F32, dbg)
    xnoscr = kb.dscr("xnoscr", [D, T_OWN], BF16, dbg)
    hscr = kb.dscr("hscr", [8192, T_OWN], BF16, dbg)
    wdscr = kb.dscr("wdscr", [8192, D], BF16, dbg)

    xin_r = xin.rearrange("(c p) t -> p c t", p=128)
    xnoscr_r = xnoscr.rearrange("(c p) t -> p c t", p=128)
    kscrn_r = kscr_n.rearrange("(c p) t -> p c t", p=128)
    qscrn_r = qscr_n.rearrange("(c p) t -> p c t", p=128)
    s0scr_r = s0scr.rearrange("(c p) t -> p c t", p=128)
    x1h_r = x1h.rearrange("(c p) t -> p c t", p=128)

    cm = {}
    cm['onesb'] = kb.sb("onesb", [128, 1], BF16)
    cm['onesf'] = kb.sb("onesf", [128, 1], F32)
    cm['onesrow'] = kb.sb("onesrow", [1, 128], F32)
    cm['h128'] = kb.sb("h128", [128, 128], BF16)
    cm['dupc'] = kb.sb("dupc", [32, 128], F32)
    cm['dups'] = kb.sb("dups", [32, 128], F32)
    cm['su_ug'] = kb.sb("su_ugS", [128, 16], F32)
    cm['su_dp'] = kb.sb("su_dpS", [128, 64], F32)
    cm['sc64ug'] = kb.sb("sc64ug", [128, 128], F32)
    cm['sc64dp'] = kb.sb("sc64dp", [128, 16], F32)
    cm['kvg'] = kb.sb("kvgS", [128, 4], F32)
    cm['kvbt'] = kb.sb("kvbtS", [128, 4], F32)
    cm['su_comb'] = kb.sb("su_comb", [128, 64], F32)
    cm['freqsT'] = kb.sb("freqsTS", [32, 2048], F32)
    xn_own = kb.sb("xn_own", [128, NCH * T_OWN], BF16)

    kb.block = kb.es.enter_context(nc.Block())
    sp = Spine(kb)
    s_ldc = kb.sem("ldc")
    s_st = kb.sem("st")
    s_ldg = kb.sem("ldg")
    s_lds = kb.sem("lds")
    chT = sp.chain('T')
    chE = sp.chain('E')

    CONSTS = sp.dma(kb.sy, s_ldc, lambda sync: [
        sync.dma_start(out=cm['onesb'][:, :], in_=c_onesb),
        sync.dma_start(out=cm['onesf'][:, :], in_=c_onesf),
        sync.dma_start(out=cm['onesrow'][:, :], in_=c_onesro),
        sync.dma_start(out=cm['h128'][:, :], in_=c_h128),
        sync.dma_start(out=cm['dupc'][:, :], in_=c_dupc),
        sync.dma_start(out=cm['dups'][:, :], in_=c_dups),
        sync.dma_start(out=cm['su_ug'][:, :], in_=su_ug),
        sync.dma_start(out=cm['su_dp'][:, :], in_=su_dp),
        sync.dma_start(out=cm['kvg'][:, :], in_=kvg),
        sync.dma_start(out=cm['kvbt'][:, :], in_=kvbt),
        sync.dma_start(out=cm['freqsT'][:, :], in_=freqsT),
        sync.dma_start(out=cm['sc64ug'][:, :], in_=wsc_ug),
        sync.dma_start(out=cm['sc64dp'][:, :], in_=wsc_dp),
    ])

    def p0s(v):
        v.tensor_scalar(out=cm['sc64ug'][:, :], in0=cm['sc64ug'][:, :],
                        scalar1=64.0, scalar2=None, op0=ALU.mult)
        v.tensor_scalar(out=cm['sc64dp'][:, :], in0=cm['sc64dp'][:, :],
                        scalar1=64.0, scalar2=None, op0=ALU.mult)
        v.tensor_tensor(out=cm['su_comb'][:, :], in0=cm['sc64ug'][:, 0:64],
                        in1=cm['su_dp'][:, :], op=ALU.mult)
        return v.tensor_scalar(out=cm['su_comb'][:, :], in0=cm['su_comb'][:, :],
                               scalar1=FSC2, scalar2=None, op0=ALU.mult)
    sp.V(p0s, extra=[(s_ldc, CONSTS)])

    pw = ExitStack()
    wdsl = kb.sb("wdsl", [128, 64 * 512], BF16, pw)
    btw = kb.sb("btw", [128, 512], BF16, pw)
    poolw = [kb.ps(f"ppw{j}", [128, 2048], F32, pw) for j in range(2)]
    wdn_r = wdownT.rearrange("(kc p) m -> p kc m", p=128)
    wdscr_rw = wdscr.rearrange("(kc p) m -> p kc m", p=128)
    for jsl in range(4):
        j0 = jsl * 512
        WLD = sp.dma(kb.g, s_ldg, lambda g, j0=j0: [g.dma_start(
            out=wdsl[:, :].rearrange("p (c m) -> p c m", c=64),
            in_=wdn_r[:, :, j0:j0 + 512])],
            extra=[(chE, chE.n), (s_st, s_st.n)])
        EVw = {}
        for cg in range(16):
            def wmm(t, cg=cg):
                ins = None
                for i in range(4):
                    c = cg * 4 + i
                    ins = t.matmul(poolw[cg % 2][:, i * 512:(i + 1) * 512],
                                   cm['h128'][:, :], wdsl[:, c * 512:(c + 1) * 512],
                                   start=True, stop=True)
                return ins
            ext = [(s_ldg, WLD)]
            ext.append((chE, EVw[cg - 2] if cg >= 2 else chE.n))
            TM = sp.T(wmm, extra=ext)
            EVw[cg] = sp.S(lambda s, cg=cg: s.activation(
                out=wdsl[:, cg * 2048:(cg + 1) * 2048],
                in_=poolw[cg % 2][:, :], func=AF.Identity), extra=[(chT, TM)])
        for stg in (1, 2, 4, 8, 16, 32):
            def bflyw(v, stg=stg):
                ins = None
                for i in range(64):
                    if i & stg:
                        continue
                    j = i | stg
                    a = wdsl[:, i * 512:(i + 1) * 512]
                    b = wdsl[:, j * 512:(j + 1) * 512]
                    v.tensor_copy(out=btw[:, :], in_=a)
                    v.tensor_tensor(out=a, in0=btw[:, :], in1=b, op=ALU.add)
                    ins = v.tensor_tensor(out=b, in0=btw[:, :], in1=b, op=ALU.subtract)
                return ins
            sp.V(bflyw)
        sp.dma(kb.sy, s_st, lambda sync, j0=j0: [sync.dma_start(
            out=wdscr_rw[:, :, j0:j0 + 512],
            in_=wdsl[:, :].rearrange("p (c m) -> p c m", c=64))],
            extra=[(chE, chE.n)])
    PW_ST = s_st.n
    pw.close()

    p1 = ExitStack()
    xno = [kb.sb(f"xno{j}", [128, NCH * 512], BF16, p1) for j in range(2)]
    xno_store_marks = {}

    def xn_dst(c, t0):
        if t0 < T_OWN:
            return xn_own[:, c * T_OWN + t0:c * T_OWN + t0 + 512]
        j = ((t0 - T_OWN) // 512) % 2
        return xno[j][:, c * 512:(c + 1) * 512]

    def pre_xn(v, i):
        if i >= 6:
            v.wait_ge(s_st.h, xno_store_marks[i - 2])

    def post_tile(i, t0):
        if i < 4:
            return
        j = (i - 4) % 2
        s_out_ln = kb.sem("lnout1")
        def xno_store(sync, i=i, t0=t0, j=j):
            sync.wait_ge(s_out_ln.h, i + 1)
            s_st.inc(sync.dma_start(
                out=xnoscr_r[:, :, t0 - T_OWN:t0 - T_OWN + 512],
                in_=xno[j][:, :].rearrange("p (c t) -> p c t", c=NCH)), 16)
        kb.sy(xno_store)
        xno_store_marks[i] = s_st.n

    s_ln = emit_ln(kb, "1", xin_r, [i * 512 for i in range(8)], xn_dst, cm, p1,
                   pre_xn=pre_xn, post_tile=post_tile, gate=[(chE, chE.n)])
    XNO_ST = s_st.n

    def p1done(v):
        v.wait_ge(s_ln.h, 8)
        v.wait_ge(s_st.h, XNO_ST)
        return v.engine_nop()
    sp.V(p1done)
    p1.close()

    p2 = ExitStack()
    wkva_cb = kb.sb("wkva_cb", [128, 16 * 512], BF16, p2)
    wkva_rb = kb.sb("wkva_rb", [128, 16 * 64], BF16, p2)
    wkvb_kb = kb.sb("wkvb_kb", [128, 4 * 2048], BF16, p2)
    wkvb_vb = kb.sb("wkvb_vb", [128, 4 * 2048], BF16, p2)
    xnsp = kb.sb("xnsp", [128, NCH * 512], BF16, p2)
    kvc_f = kb.sb("kvc_f", [128, 4 * 512], F32, p2)
    sq2 = kb.sb("sq2", [128, 4 * 512], F32, p2)
    kvcn = kb.sb("kvcn", [128, 4 * 512], BF16, p2)
    kr_raw = kb.sb("kr_raw", [64, 512], BF16, p2)
    kr_sw = kb.sb("kr_sw", [64, 512], BF16, p2)
    kr_m1 = kb.sb("kr_m1", [64, 512], F32, p2)
    kr_out = kb.sb("kr_out", [64, 512], BF16, p2)
    rows2 = kb.sb("rows2", [1, 2048], F32, p2)
    csf = kb.sb("csf", [64, 2 * 512], F32, p2)
    tht = kb.sb("tht", [32, 3 * 512], F32, p2)
    kn4 = kb.sb("kn4", [128, 4 * 512], BF16, p2)
    v_b = kb.sb("v_b", [128, 2048], BF16, p2)
    ps_a = kb.ps("ps2_a", [128, 2048], F32, p2)
    ps_rope = kb.ps("ps2_rope", [64, 512], F32, p2)
    ps_st2 = kb.ps("ps2_st", [1, 1024], F32, p2)
    ps_cc = ps_a

    WKV = sp.dma(kb.g, s_ldg, lambda g: [
        g.dma_start(out=wkva_cb[:, :].rearrange("p (kc m) -> p kc m", kc=16),
                    in_=wkva_c.rearrange("(kc p) m -> p kc m", p=128)),
        g.dma_start(out=wkva_rb[:, :].rearrange("p (kc m) -> p kc m", kc=16),
                    in_=wkva_r.rearrange("(kc p) m -> p kc m", p=128)),
        g.dma_start(out=wkvb_kb[:, :].rearrange("p (kc m) -> p kc m", kc=4),
                    in_=wkvb_k.rearrange("(kc p) m -> p kc m", p=128)),
        g.dma_start(out=wkvb_vb[:, :].rearrange("p (kc m) -> p kc m", kc=4),
                    in_=wkvb_v.rearrange("(kc p) m -> p kc m", p=128)),
    ], extra=[(chE, chE.n)])

    for spn in range(8):
        t0 = spn * 512
        own = t0 < T_OWN
        if not own:
            XNSP = sp.dma(kb.sy, s_lds, lambda sync, t0=t0: [sync.dma_start(
                out=xnsp[:, :].rearrange("p (c t) -> p c t", c=NCH),
                in_=xnoscr_r[:, :, t0 - T_OWN:t0 - T_OWN + 512])],
                extra=[(chT, chT.n)])

        def xn_rd(kc, t0=t0, own=own):
            if own:
                return xn_own[:, kc * T_OWN + t0:kc * T_OWN + t0 + 512]
            return xnsp[:, kc * 512:(kc + 1) * 512]

        extra = [(s_ldg, WKV), (chE, chE.n)]
        if not own:
            extra.append((s_lds, XNSP))

        def kva_mms(t, xn_rd=xn_rd):
            for oc in range(4):
                for kc in range(16):
                    t.matmul(ps_a[:, oc * 512:(oc + 1) * 512],
                             wkva_cb[:, kc * 512 + oc * 128:kc * 512 + oc * 128 + 128],
                             xn_rd(kc), start=(kc == 0), stop=(kc == 15))
            ins = None
            for kc in range(16):
                ins = t.matmul(ps_rope[:, :], wkva_rb[:, kc * 64:(kc + 1) * 64],
                               xn_rd(kc), start=(kc == 0), stop=(kc == 15))
            return ins
        TM = sp.T(kva_mms, extra=extra)

        def kva_evac(s):
            for oc in range(4):
                s.activation(out=kvc_f[:, oc * 512:(oc + 1) * 512],
                             in_=ps_a[:, oc * 512:(oc + 1) * 512], func=AF.Identity)
            s.activation(out=kr_raw[:, :], in_=ps_rope[:, :], func=AF.Identity)
            ins = None
            for oc in range(4):
                ins = s.activation(out=sq2[:, oc * 512:(oc + 1) * 512],
                                   in_=kvc_f[:, oc * 512:(oc + 1) * 512], func=AF.Square)
            return ins
        EV = sp.S(kva_evac, extra=[(chT, TM)])

        def kv_stats(t):
            for oc in range(4):
                t.matmul(ps_st2[0:1, 0:512], cm['onesf'][:, :],
                         kvc_f[:, oc * 512:(oc + 1) * 512],
                         start=(oc == 0), stop=(oc == 3))
            ins = None
            for oc in range(4):
                ins = t.matmul(ps_st2[0:1, 512:1024], cm['onesf'][:, :],
                               sq2[:, oc * 512:(oc + 1) * 512],
                               start=(oc == 0), stop=(oc == 3))
            return ins
        TM = sp.T(kv_stats, extra=[(chE, EV)])

        def kv_rows(v):
            v.tensor_scalar(out=rows2[0:1, 0:512], in0=ps_st2[0:1, 0:512],
                            scalar1=1.0 / 512, scalar2=None, op0=ALU.mult)
            v.tensor_scalar(out=rows2[0:1, 512:1024], in0=ps_st2[0:1, 512:1024],
                            scalar1=1.0 / 512, scalar2=None, op0=ALU.mult)
            v.tensor_tensor(out=rows2[0:1, 1024:1536], in0=rows2[0:1, 0:512],
                            in1=rows2[0:1, 0:512], op=ALU.mult)
            v.tensor_tensor(out=rows2[0:1, 1024:1536], in0=rows2[0:1, 512:1024],
                            in1=rows2[0:1, 1024:1536], op=ALU.subtract)
            return v.tensor_scalar(out=rows2[0:1, 1024:1536], in0=rows2[0:1, 1024:1536],
                                   scalar1=1e-5, scalar2=None, op0=ALU.add)
        sp.V(kv_rows, extra=[(chT, TM)])

        sp.S(lambda s: s.activation(out=rows2[0:1, 1536:2048], in_=rows2[0:1, 1024:1536],
                                    func=AF.Sqrt))

        def kv_rows2(v):
            v.reciprocal(out=rows2[0:1, 1024:1536], in_=rows2[0:1, 1536:2048])
            return v.tensor_tensor(out=rows2[0:1, 1536:2048], in0=rows2[0:1, 0:512],
                                   in1=rows2[0:1, 1024:1536], op=ALU.mult)
        EV = sp.V(kv_rows2)

        def kv_bcast(t):
            t.matmul(ps_cc[:, 0:512], cm['onesrow'][0:1, :], rows2[0:1, 1024:1536],
                     start=True, stop=True)
            return t.matmul(ps_cc[:, 512:1024], cm['onesrow'][0:1, :],
                            rows2[0:1, 1536:2048], start=True, stop=True)
        TM = sp.T(kv_bcast, extra=[(chE, EV)])

        def kv_norm(v):
            ins = None
            for oc in range(4):
                v.tensor_tensor(out=sq2[:, oc * 512:(oc + 1) * 512],
                                in0=kvc_f[:, oc * 512:(oc + 1) * 512],
                                in1=ps_cc[:, 0:512], op=ALU.mult)
                v.tensor_tensor(out=sq2[:, oc * 512:(oc + 1) * 512],
                                in0=sq2[:, oc * 512:(oc + 1) * 512],
                                in1=ps_cc[:, 512:1024], op=ALU.subtract)
                ins = v.tensor_scalar(out=kvcn[:, oc * 512:(oc + 1) * 512],
                                      in0=sq2[:, oc * 512:(oc + 1) * 512],
                                      scalar1=cm['kvg'][:, oc:oc + 1],
                                      scalar2=cm['kvbt'][:, oc:oc + 1],
                                      op0=ALU.mult, op1=ALU.add)
            return ins
        sp.V(kv_norm, extra=[(chT, TM)])

        def rope_theta(v, t0=t0):
            th0 = t0 % 2048
            v.tensor_copy(out=tht[:, 0:512], in_=cm['freqsT'][:, th0:th0 + 512])
            v.tensor_scalar(out=tht[:, 1024:1536], in0=tht[:, 0:512],
                            scalar1=PI, scalar2=None, op0=ALU.is_gt)
            v.scalar_tensor_tensor(out=tht[:, 0:512], in0=tht[:, 1024:1536],
                                   scalar=-2 * PI, in1=tht[:, 0:512],
                                   op0=ALU.mult, op1=ALU.add)
            v.tensor_scalar(out=tht[:, 512:1024], in0=tht[:, 0:512],
                            scalar1=-1.0, scalar2=PI / 2, op0=ALU.mult, op1=ALU.add)
            v.tensor_scalar(out=tht[:, 1024:1536], in0=tht[:, 512:1024],
                            scalar1=PI, scalar2=None, op0=ALU.is_gt)
            return v.scalar_tensor_tensor(out=tht[:, 512:1024], in0=tht[:, 1024:1536],
                                          scalar=-2 * PI, in1=tht[:, 512:1024],
                                          op0=ALU.mult, op1=ALU.add)
        sp.V(rope_theta)

        def rope_sin(s):
            s.activation(out=tht[:, 0:512], in_=tht[:, 0:512], func=AF.Sin)
            return s.activation(out=tht[:, 512:1024], in_=tht[:, 512:1024], func=AF.Sin)
        EV = sp.S(rope_sin)

        def rope_dup(t):
            t.matmul(ps_cc[0:64, 0:512], cm['dupc'][:, 0:64], tht[:, 512:1024],
                     start=True, stop=True)
            return t.matmul(ps_cc[0:64, 512:1024], cm['dups'][:, 0:64], tht[:, 0:512],
                            start=True, stop=True)
        TM = sp.T(rope_dup, extra=[(chE, EV)])

        def rope_evac(s):
            s.activation(out=csf[:, 0:512], in_=ps_cc[0:64, 0:512], func=AF.Identity)
            return s.activation(out=csf[:, 512:1024], in_=ps_cc[0:64, 512:1024],
                                func=AF.Identity)
        sp.S(rope_evac, extra=[(chT, TM)])

        STPREV = s_st.n
        def rope_apply(v):
            v.stream_shuffle(out=kr_sw[:, :], in_=kr_raw[:, :], mask=SWAP_MASK)
            v.tensor_tensor(out=kr_m1[:, :], in0=kr_raw[:, :], in1=csf[:, 0:512],
                            op=ALU.mult)
            v.tensor_tensor(out=kr_out[:, :], in0=kr_sw[:, :], in1=csf[:, 512:1024],
                            op=ALU.mult)
            return v.tensor_tensor(out=kr_out[:, :], in0=kr_m1[:, :], in1=kr_out[:, :],
                                   op=ALU.add)
        EV = sp.V(rope_apply, extra=[(s_st, STPREV)])

        sp.dma(kb.sy, s_st, lambda sync, t0=t0: [
            sync.dma_start(out=kscr_r[:, t0:t0 + 512], in_=kr_out[:, :])],
            extra=[(chE, EV)])

        for grp in range(4):
            def kn_mms(t, grp=grp):
                ins = None
                for j in range(4):
                    oc = grp * 4 + j
                    for kc in range(4):
                        ins = t.matmul(
                            ps_a[:, j * 512:(j + 1) * 512],
                            wkvb_kb[:, kc * 2048 + oc * 128:kc * 2048 + oc * 128 + 128],
                            kvcn[:, kc * 512:(kc + 1) * 512],
                            start=(kc == 0), stop=(kc == 3))
                return ins
            TM = sp.T(kn_mms, extra=[(chE, chE.n)])

            STPREV = s_st.n
            def kn_evac(s):
                ins = None
                for j in range(4):
                    ins = s.activation(out=kn4[:, j * 512:(j + 1) * 512],
                                       in_=ps_a[:, j * 512:(j + 1) * 512],
                                       func=AF.Identity)
                return ins
            EV = sp.S(kn_evac, extra=[(chT, TM), (s_st, STPREV)])

            sp.dma(kb.sy, s_st, lambda sync, grp=grp, t0=t0: [sync.dma_start(
                out=kscrn_r[:, grp * 4:(grp + 1) * 4, t0:t0 + 512],
                in_=kn4[:, :].rearrange("p (c t) -> p c t", c=4))],
                extra=[(chE, EV)])

        for st_i in range(4):
            def v_mms(t, st_i=st_i):
                ins = None
                for h in range(16):
                    for kc in range(4):
                        ins = t.matmul(
                            ps_a[:, h * 128:(h + 1) * 128],
                            kvcn[:, kc * 512 + st_i * 128:kc * 512 + st_i * 128 + 128],
                            wkvb_vb[:, kc * 2048 + h * 128:kc * 2048 + h * 128 + 128],
                            start=(kc == 0), stop=(kc == 3))
                return ins
            TM = sp.T(v_mms, extra=[(chE, chE.n)])

            STPREV = s_st.n
            EV = sp.S(lambda s: s.activation(out=v_b[:, :], in_=ps_a[:, :],
                                             func=AF.Identity),
                      extra=[(chT, TM), (s_st, STPREV)])

            sp.dma(kb.sy, s_st, lambda sync, st_i=st_i, t0=t0: [sync.dma_start(
                out=vscr[t0 + st_i * 128:t0 + st_i * 128 + 128, :], in_=v_b[:, :])],
                extra=[(chE, EV)])
    P2_ST = s_st.n
    p2.close()

    def gemm_phase(tag, wsrc, KC, MC, a_rd, nsub, stack, epilogue, colw=128,
                   npools=2, pool_cols=None, first_extra=()):
        wr = wsrc.rearrange("(kc p) m -> p kc m", p=128)
        wbuf = [kb.sb(f"wb{tag}{j}", [128, KC * colw], BF16, stack) for j in range(2)]
        pc = pool_cols or (nsub * 512)
        pools = [kb.ps(f"pp{tag}{j}", [128, pc], F32, stack) for j in range(npools)]
        TMm, EVm = {}, {}

        def wload(oc, extra):
            return sp.dma(kb.g, s_ldg, lambda g, oc=oc: [g.dma_start(
                out=wbuf[oc % 2][:, :].rearrange("p (kc m) -> p kc m", kc=KC),
                in_=wr[:, :, oc * colw:(oc + 1) * colw])], extra=extra)

        base_gate = [(chT, chT.n), (chE, chE.n)] + list(first_extra)
        WQ = {0: wload(0, list(base_gate))}
        if MC > 1:
            WQ[1] = wload(1, list(base_gate))
        for oc in range(MC):
            pool = pools[oc % npools]
            ext = [(s_ldg, WQ[oc])]
            if oc >= npools:
                ext.append((chE, EVm[oc - npools]))

            def mms(t, oc=oc, pool=pool):
                ins = None
                for kc in range(KC):
                    for ts in range(nsub):
                        ins = t.matmul(
                            pool[:, ts * 512:(ts + 1) * 512],
                            wbuf[oc % 2][:, kc * colw:kc * colw + 128],
                            a_rd(kc, ts), start=(kc == 0), stop=(kc == KC - 1))
                return ins
            TMm[oc] = sp.T(mms, extra=ext)
            if oc + 2 < MC:
                WQ[oc + 2] = wload(oc + 2, [(chT, TMm[oc])])
            EVm[oc] = epilogue(oc, pool, TMm[oc])
        return TMm, EVm

    p3a = ExitStack()
    csq_c = kb.sb("csq_c", [128, 4 * 512], F32, p3a)
    csq_s = kb.sb("csq_s", [128, 4 * 512], F32, p3a)
    tht3 = kb.sb("tht3", [32, 3 * 512], F32, p3a)
    p3cs = ExitStack()
    ps3 = kb.ps("ps3cc", [128, 1024], F32, p3cs)

    for tt in range(4):
        def q_theta(v, tt=tt):
            th0 = tt * 512
            v.tensor_copy(out=tht3[:, 0:512], in_=cm['freqsT'][:, th0:th0 + 512])
            v.tensor_scalar(out=tht3[:, 1024:1536], in0=tht3[:, 0:512],
                            scalar1=PI, scalar2=None, op0=ALU.is_gt)
            v.scalar_tensor_tensor(out=tht3[:, 0:512], in0=tht3[:, 1024:1536],
                                   scalar=-2 * PI, in1=tht3[:, 0:512],
                                   op0=ALU.mult, op1=ALU.add)
            v.tensor_scalar(out=tht3[:, 512:1024], in0=tht3[:, 0:512],
                            scalar1=-1.0, scalar2=PI / 2, op0=ALU.mult, op1=ALU.add)
            v.tensor_scalar(out=tht3[:, 1024:1536], in0=tht3[:, 512:1024],
                            scalar1=PI, scalar2=None, op0=ALU.is_gt)
            return v.scalar_tensor_tensor(out=tht3[:, 512:1024], in0=tht3[:, 1024:1536],
                                          scalar=-2 * PI, in1=tht3[:, 512:1024],
                                          op0=ALU.mult, op1=ALU.add)
        sp.V(q_theta)

        def q_sin(s):
            s.activation(out=tht3[:, 0:512], in_=tht3[:, 0:512], func=AF.Sin)
            return s.activation(out=tht3[:, 512:1024], in_=tht3[:, 512:1024], func=AF.Sin)
        EV = sp.S(q_sin)

        def q_dup(t):
            t.matmul(ps3[:, 0:512], cm['dupc'][:, :], tht3[:, 512:1024],
                     start=True, stop=True)
            return t.matmul(ps3[:, 512:1024], cm['dups'][:, :], tht3[:, 0:512],
                            start=True, stop=True)
        TM = sp.T(q_dup, extra=[(chE, EV)])

        def q_csevac(s, tt=tt):
            s.activation(out=csq_c[:, tt * 512:(tt + 1) * 512], in_=ps3[:, 0:512],
                         func=AF.Identity)
            return s.activation(out=csq_s[:, tt * 512:(tt + 1) * 512],
                                in_=ps3[:, 512:1024], func=AF.Identity)
        sp.S(q_csevac, extra=[(chT, TM)])
    p3cs.close()

    p3b = ExitStack()
    qn_b = [kb.sb(f"qn_b{j}", [128, 2048], BF16, p3b) for j in range(2)]
    qr_b = kb.sb("qr_b", [128, 512], BF16, p3b)
    qr_sw = kb.sb("qr_sw", [128, 512], BF16, p3b)
    qr_m1 = kb.sb("qr_m1", [128, 512], F32, p3b)
    qr_o = [kb.sb(f"qr_o{j}", [128, 512], BF16, p3b) for j in range(2)]

    def xn_a(kc, ts):
        return xn_own[:, kc * T_OWN + ts * 512:kc * T_OWN + ts * 512 + 512]

    st_marks = {}

    def qn_epi(oc, pool, TM):
        STP = st_marks.get(oc - 2, 0)
        def ev(s, oc=oc, pool=pool):
            return s.activation(out=qn_b[oc % 2][:, :], in_=pool[:, :], func=AF.Identity)
        EV = sp.S(ev, extra=[(chT, TM), (s_st, STP)])
        sp.dma(kb.sy, s_st, lambda sync, oc=oc: [sync.dma_start(
            out=qscrn_r[:, oc, :], in_=qn_b[oc % 2][:, :])], extra=[(chE, EV)])
        st_marks[oc] = s_st.n
        return EV

    p3b1 = ExitStack()
    gemm_phase("qn", wq_n, 16, 16, xn_a, 4, p3b1, qn_epi,
               first_extra=[(chE, chE.n), (chT, chT.n)])
    p3b1.close()

    str_marks = {}

    def qr_epi(oc, pool, TM):
        STP = str_marks.get(oc - 2, 0)
        lastEV = None
        for ts in range(4):
            def ev(s, oc=oc, pool=pool, ts=ts):
                return s.activation(out=qr_b[:, :], in_=pool[:, ts * 512:(ts + 1) * 512],
                                    func=AF.Identity)
            sp.S(ev, extra=([(chT, TM)] if ts == 0 else []) + [(s_st, STP)])

            def rope(v, oc=oc, ts=ts):
                v.stream_shuffle(out=qr_sw[:, :], in_=qr_b[:, :], mask=SWAP_MASK)
                v.tensor_tensor(out=qr_m1[:, :], in0=qr_b[:, :],
                                in1=csq_c[:, ts * 512:(ts + 1) * 512], op=ALU.mult)
                v.tensor_tensor(out=qr_o[oc % 2][:, :], in0=qr_sw[:, :],
                                in1=csq_s[:, ts * 512:(ts + 1) * 512], op=ALU.mult)
                return v.tensor_tensor(out=qr_o[oc % 2][:, :], in0=qr_m1[:, :],
                                       in1=qr_o[oc % 2][:, :], op=ALU.add)
            lastEV = sp.V(rope)
            sp.dma(kb.sy, s_st, lambda sync, oc=oc, ts=ts: [sync.dma_start(
                out=qscr_r[oc * 128:(oc + 1) * 128, ts * 512:(ts + 1) * 512],
                in_=qr_o[oc % 2][:, :])], extra=[(chE, lastEV)])
        str_marks[oc] = s_st.n
        return lastEV

    p3b2 = ExitStack()
    gemm_phase("qr", wq_r, 16, 8, xn_a, 4, p3b2, qr_epi)
    p3b2.close()
    P3_ST = s_st.n
    p3b.close()
    p3a.close()

    p4 = ExitStack()
    kn_h = kb.sb("kn_h", [128, 4096], BF16, p4)
    kr_all = kb.sb("kr_all", [64, 4096], BF16, p4)
    vh = kb.sb("vh", [128, 32 * 128], BF16, p4)
    qn_t = kb.sb("qn_t", [128, 512], BF16, p4)
    qr_t = kb.sb("qr_t", [64, 512], BF16, p4)
    es_t = [kb.sb(f"es{j}", [128, 512], BF16, p4) for j in range(2)]
    rrow = kb.sb("rrow", [1, 512], F32, p4)
    rb_t = kb.sb("rb_t", [128, 512], F32, p4)
    atile = [kb.sb(f"at{j}", [128, 512], BF16, p4) for j in range(2)]
    ps_s = [kb.ps(f"ps4s{j}", [128, 512], F32, p4) for j in range(2)]
    ps_pv = kb.ps("ps4pv", [128, 512], F32, p4)
    ps_row = kb.ps("ps4row", [1, 512], F32, p4)
    ps_bc = kb.ps("ps4bc", [128, 512], F32, p4)
    s4_ld = kb.sem("s4ld")
    s4_smm = kb.sem("s4smm")
    s4_exp = kb.sem("s4exp")
    s4_pvd = kb.sem("s4pvd")
    s4_f = kb.sem("s4f")
    s4_fin = kb.sem("s4fin")
    vscr_r = vscr.rearrange("(c p) d -> p c d", p=128)

    KRL = sp.dma(kb.sy, s4_ld, lambda sync: [
        sync.dma_start(out=kr_all[:, :], in_=kscr_r)],
        extra=[(s_st, P2_ST), (chT, chT.n), (chE, chE.n)])

    atile_marks = {}
    for h in range(16):
        ex = [(s_st, P3_ST if h == 0 else 0)]
        if h >= 1:
            ex.append((s4_fin, 4 * h))
        KLD = sp.dma(kb.sy, s4_ld, lambda sync, h=h: [
            sync.dma_start(out=kn_h[:, :], in_=kscr_n[h * 128:(h + 1) * 128, :]),
            sync.dma_start(out=vh[:, :].rearrange("p (c d) -> p c d", c=32),
                           in_=vscr_r[:, :, h * 128:(h + 1) * 128])], extra=ex)

        for qt in range(4):
            i_hq = h * 4 + qt
            QPREV = s4_smm.n
            QLD = sp.dma(kb.sy, s4_ld, lambda sync, h=h, qt=qt: [
                sync.dma_start(out=qn_t[:, :],
                               in_=qscr_n[h * 128:(h + 1) * 128, qt * 512:(qt + 1) * 512]),
                sync.dma_start(out=qr_t[:, :],
                               in_=qscr_r[h * 64:(h + 1) * 64, qt * 512:(qt + 1) * 512])],
                extra=[(s4_smm, QPREV)])

            E0 = s4_exp.n
            M0 = s4_smm.n

            def fmms(t, h=h, qt=qt, i_hq=i_hq, E0=E0, KLD=KLD, QLD=QLD):
                t.wait_ge(s4_ld.h, max(KLD, QLD, KRL))
                if i_hq >= 1:
                    t.wait_ge(s4_fin.h, i_hq)
                for s in range(32):
                    if s >= 1:
                        t.wait_ge(s4_exp.h, E0 + s)
                        t.matmul(ps_row[0:1, :], cm['onesb'][:, :], es_t[(s - 1) % 2][:, :],
                                 start=(s == 1), stop=(s == 32))
                        t.matmul(ps_pv[:, :], vh[:, (s - 1) * 128:s * 128],
                                 es_t[(s - 1) % 2][:, :], start=(s == 1), stop=(s == 32))
                    t.matmul(ps_s[s % 2][:, :], kn_h[:, s * 128:(s + 1) * 128],
                             qn_t[:, :], start=True, stop=False)
                    ins = t.matmul(ps_s[s % 2][:, :], kr_all[:, s * 128:(s + 1) * 128],
                                   qr_t[:, :], start=False, stop=True)
                    s4_smm.inc(ins)
                t.wait_ge(s4_exp.h, E0 + 32)
                t.matmul(ps_row[0:1, :], cm['onesb'][:, :], es_t[31 % 2][:, :],
                         start=False, stop=True)
                ins = t.matmul(ps_pv[:, :], vh[:, 31 * 128:32 * 128], es_t[31 % 2][:, :],
                               start=False, stop=True)
                s4_pvd.inc(ins)
            kb.t(fmms)

            def fexp(s, M0=M0, E0=E0):
                ins = None
                for si in range(32):
                    s.wait_ge(s4_smm.h, M0 + si + 1)
                    ins = s.activation(out=es_t[si % 2][:, :], in_=ps_s[si % 2][:, :],
                                       func=AF.Exp, scale=SCALE_L0)
                    s4_exp.inc(ins)
            kb.s(fexp)

            F0 = s4_f.n
            def frecip(v, i_hq=i_hq):
                v.wait_ge(s4_pvd.h, i_hq + 1)
                ins = v.reciprocal(out=rrow[0:1, :], in_=ps_row[0:1, :])
                s4_f.inc(ins)
            kb.v(frecip)

            def fbc(t, F0=F0):
                t.wait_ge(s4_f.h, F0 + 1)
                ins = t.matmul(ps_bc[:, :], cm['onesrow'][0:1, :], rrow[0:1, :],
                               start=True, stop=True)
                s4_f.inc(ins)
            kb.t(fbc)

            def frb(s, F0=F0):
                s.wait_ge(s4_f.h, F0 + 2)
                ins = s.activation(out=rb_t[:, :], in_=ps_bc[:, :], func=AF.Identity)
                s4_f.inc(ins)
            kb.s(frb)

            ATP = atile_marks.get(i_hq - 2, 0)
            def fattn(v, F0=F0, qt=qt, ATP=ATP):
                v.wait_ge(s4_f.h, F0 + 3)
                if ATP > 0:
                    v.wait_ge(s_st.h, ATP)
                ins = v.tensor_tensor(out=atile[qt % 2][:, :], in0=ps_pv[:, :],
                                      in1=rb_t[:, :], op=ALU.mult)
                s4_fin.inc(ins)
            kb.v(fattn)

            sp.dma(kb.sy, s_st, lambda sync, h=h, qt=qt: [sync.dma_start(
                out=attnscr[h * 128:(h + 1) * 128, qt * 512:(qt + 1) * 512],
                in_=atile[qt % 2][:, :])], extra=[(s4_fin, i_hq + 1)])
            atile_marks[i_hq] = s_st.n
    P4_ST = s_st.n
    p4.close()
    def p4done(v):
        v.wait_ge(s4_fin.h, 64)
        return v.engine_nop()
    sp.V(p4done)

    p5 = ExitStack()
    atl = kb.sb("atl", [128, 16 * 1024], BF16, p5)
    so_t = [kb.sb(f"so{j}", [128, 1024], F32, p5) for j in range(2)]
    wbo = [kb.sb(f"wbo{j}", [128, 16 * 128], BF16, p5) for j in range(2)]
    pso = [kb.ps(f"pso{j}", [128, 1024], F32, p5) for j in range(2)]
    wout_r = wout.rearrange("(kc p) m -> p kc m", p=128)
    attn_r = attnscr.rearrange("(c p) t -> p c t", p=128)

    so_marks = {}
    TMm, EVm = {}, {}
    for th in range(2):
        ATL = sp.dma(kb.sy, s_lds, lambda sync, th=th: [sync.dma_start(
            out=atl[:, :].rearrange("p (c t) -> p c t", c=16),
            in_=attn_r[:, :, th * 1024:(th + 1) * 1024])],
            extra=[(s_st, P4_ST), (chT, chT.n)])
        for oc in range(16):
            gi = th * 16 + oc
            WLD = sp.dma(kb.g, s_ldg, lambda g, oc=oc, gi=gi: [g.dma_start(
                out=wbo[gi % 2][:, :].rearrange("p (kc m) -> p kc m", kc=16),
                in_=wout_r[:, :, oc * 128:(oc + 1) * 128])],
                extra=[(chT, TMm.get(gi - 2, 0))] if gi >= 2 else [(chE, chE.n)])

            def wmms(t, gi=gi):
                ins = None
                for kc in range(16):
                    for ts in range(2):
                        ins = t.matmul(pso[gi % 2][:, ts * 512:(ts + 1) * 512],
                                       wbo[gi % 2][:, kc * 128:(kc + 1) * 128],
                                       atl[:, kc * 1024 + ts * 512:kc * 1024 + ts * 512 + 512],
                                       start=(kc == 0), stop=(kc == 15))
                return ins
            ext = [(s_ldg, WLD), (s_lds, ATL)]
            if gi >= 2:
                ext.append((chE, EVm[gi - 2]))
            TMm[gi] = sp.T(wmms, extra=ext)

            def sevac(s, gi=gi):
                return s.activation(out=so_t[gi % 2][:, :], in_=pso[gi % 2][:, :],
                                    func=AF.Identity)
            EVm[gi] = sp.S(sevac, extra=[(chT, TMm[gi]), (s_st, so_marks.get(gi - 2, 0))])
            sp.dma(kb.sy, s_st, lambda sync, oc=oc, th=th, gi=gi: [sync.dma_start(
                out=s0scr_r[:, oc, th * 1024:(th + 1) * 1024], in_=so_t[gi % 2][:, :])],
                extra=[(chE, EVm[gi])])
            so_marks[gi] = s_st.n
    P5_ST = s_st.n
    p5.close()

    p6 = ExitStack()
    btmp = kb.sb("btmp", [128, 2048], BF16, p6)
    pool6 = [kb.ps(f"pp6{j}", [128, 2048], F32, p6) for j in range(2)]

    def sgn1(v):
        ins = None
        for c in range(NCH):
            ins = v.tensor_scalar(out=xn_own[:, c * T_OWN:(c + 1) * T_OWN],
                                  in0=xn_own[:, c * T_OWN:(c + 1) * T_OWN],
                                  scalar1=cm['su_ug'][:, c:c + 1], scalar2=FSC1,
                                  op0=ALU.mult, op1=ALU.mult)
        return ins
    sp.V(sgn1, extra=[(chT, chT.n)])

    EV6 = {}
    for c in range(NCH):
        def h1mm(t, c=c):
            ins = None
            for ts in range(4):
                ins = t.matmul(pool6[c % 2][:, ts * 512:(ts + 1) * 512],
                               cm['h128'][:, :],
                               xn_own[:, c * T_OWN + ts * 512:c * T_OWN + ts * 512 + 512],
                               start=True, stop=True)
            return ins
        ext = [(chE, chE.n)] if c < 2 else [(chE, EV6[c - 2])]
        TM = sp.T(h1mm, extra=ext)
        EV6[c] = sp.S(lambda s, c=c: s.activation(
            out=xn_own[:, c * T_OWN:(c + 1) * T_OWN], in_=pool6[c % 2][:, :],
            func=AF.Identity), extra=[(chT, TM)])

    for stg in (1, 2, 4, 8):
        def bfly(v, stg=stg):
            ins = None
            for i in range(NCH):
                if i & stg:
                    continue
                j = i | stg
                a = xn_own[:, i * T_OWN:(i + 1) * T_OWN]
                b = xn_own[:, j * T_OWN:(j + 1) * T_OWN]
                v.tensor_copy(out=btmp[:, :], in_=a)
                v.tensor_tensor(out=a, in0=btmp[:, :], in1=b, op=ALU.add)
                ins = v.tensor_tensor(out=b, in0=btmp[:, :], in1=b, op=ALU.subtract)
            return ins
        sp.V(bfly)
    p6.close()

    p7 = ExitStack()
    hp = [kb.sb(f"hp{j}", [128, 2048], BF16, p7) for j in range(2)]
    sil = kb.sb("sil", [128, 2048], F32, p7)

    def ug_ard(kc, ts):
        return xn_own[:, kc * T_OWN + ts * 512:kc * T_OWN + ts * 512 + 512]

    hst_marks = {}

    def ug_epi(p, pool, TM):
        def dosil(s, p=p, pool=pool):
            return s.activation(out=sil[:, :], in_=pool[:, 2048:4096], func=AF.Silu,
                                scale=cm['sc64ug'][:, 64 + p:64 + p + 1])
        sp.S(dosil, extra=[(chT, TM), (s_st, hst_marks.get(p - 2, 0))])

        def doh(v, p=p, pool=pool):
            return v.scalar_tensor_tensor(
                out=hp[p % 2][:, :], in0=pool[:, 0:2048],
                scalar=cm['su_comb'][:, p:p + 1], in1=sil[:, :],
                op0=ALU.mult, op1=ALU.mult)
        EV = sp.V(doh)
        sp.dma(kb.sy, s_st, lambda sync, p=p: [sync.dma_start(
            out=hscr[p * 128:(p + 1) * 128, :], in_=hp[p % 2][:, :])],
            extra=[(chE, EV)])
        hst_marks[p] = s_st.n
        return EV

    p7g = ExitStack()
    wr_ug = wug.rearrange("(kc p) m -> p kc m", p=128)
    wbu = [kb.sb(f"wbu{j}", [128, 16 * 256], BF16, p7g) for j in range(2)]
    pool7 = kb.ps("pp7", [128, 4096], F32, p7g)
    TM7, EV7, WQ7 = {}, {}, {}

    def ug_wload(p, extra):
        return sp.dma(kb.g, s_ldg, lambda g, p=p: [g.dma_start(
            out=wbu[p % 2][:, :].rearrange("p (kc m) -> p kc m", kc=16),
            in_=wr_ug[:, :, p * 256:(p + 1) * 256])], extra=extra)

    WQ7[0] = ug_wload(0, [(chE, chE.n)])
    WQ7[1] = ug_wload(1, [(chE, chE.n)])
    for p in range(64):
        ext = [(s_ldg, WQ7[p])]
        if p >= 1:
            ext.append((chE, EV7[p - 1]))
        else:
            ext.append((chE, chE.n))

        def ugmms(t, p=p):
            ins = None
            for kc in range(16):
                for ts in range(4):
                    t.matmul(pool7[:, ts * 512:(ts + 1) * 512],
                             wbu[p % 2][:, kc * 256:kc * 256 + 128],
                             ug_ard(kc, ts), start=(kc == 0), stop=(kc == 15))
                    ins = t.matmul(pool7[:, 2048 + ts * 512:2048 + ts * 512 + 512],
                                   wbu[p % 2][:, kc * 256 + 128:kc * 256 + 256],
                                   ug_ard(kc, ts), start=(kc == 0), stop=(kc == 15))
            return ins
        TM7[p] = sp.T(ugmms, extra=ext)
        if p + 2 < 64:
            WQ7[p + 2] = ug_wload(p + 2, [(chT, TM7[p])])
        EV7[p] = ug_epi(p, pool7, TM7[p])
    P7_ST = s_st.n
    p7g.close()
    p7.close()

    p9 = ExitStack()
    hsl = kb.sb("hsl", [128, 64 * 512], BF16, p9)
    wbd = [kb.sb(f"wbd{j}", [128, 64 * 128], BF16, p9) for j in range(2)]
    g_t = kb.sb("g_t", [128, 512], F32, p9)
    s0_t = [kb.sb(f"s0t{j}", [128, 512], F32, p9) for j in range(2)]
    x_t = [kb.sb(f"xt{j}", [128, 512], F32, p9) for j in range(2)]
    ot = [kb.sb(f"ot{j}", [128, 512], F32, p9) for j in range(2)]
    pool9 = [kb.ps(f"pp9{j}", [128, 512], F32, p9) for j in range(2)]
    hscr_r = hscr.rearrange("(c p) t -> p c t", p=128)
    wdscr_r = wdscr.rearrange("(c p) m -> p c m", p=128)

    res_marks = {}
    out_marks = {}
    gi9 = 0
    WQ9, TM9, EV9 = {}, {}, {}
    for tsl in range(4):
        t0 = tsl * 512
        HSL = sp.dma(kb.sy, s_lds, lambda sync, t0=t0: [sync.dma_start(
            out=hsl[:, :].rearrange("p (c t) -> p c t", c=64),
            in_=hscr_r[:, :, t0:t0 + 512])],
            extra=[(s_st, P7_ST), (chT, chT.n)])
        for oc in range(16):
            gi = gi9; gi9 += 1
            exw = [(s_st, PW_ST)] if gi < 2 else [(chT, TM9[gi - 2])]
            WQ9[gi] = sp.dma(kb.sy, s_lds, lambda sync, oc=oc, gi=gi: [sync.dma_start(
                out=wbd[gi % 2][:, :].rearrange("p (c m) -> p c m", c=64),
                in_=wdscr_r[:, :, oc * 128:(oc + 1) * 128])], extra=exw)
            exr = [(s_st, P5_ST)] if gi < 2 else [(chE, EV9[gi - 2])]
            res_marks[gi] = sp.dma(kb.sy, s_lds,
                                   lambda sync, oc=oc, t0=t0, gi=gi: [
                sync.dma_start(out=s0_t[gi % 2][:, :],
                               in_=s0scr_r[:, oc, t0:t0 + 512]),
                sync.dma_start(out=x_t[gi % 2][:, :],
                               in_=xin_r[:, oc, t0:t0 + 512])], extra=exr)

            ext = [(s_lds, max(WQ9[gi], res_marks[gi], HSL))]
            ext.append((chE, EV9[gi - 2] if gi >= 2 else chE.n))

            def dnmms(t, gi=gi):
                ins = None
                for kc in range(64):
                    ins = t.matmul(pool9[gi % 2][:, :],
                                   wbd[gi % 2][:, kc * 128:(kc + 1) * 128],
                                   hsl[:, kc * 512:(kc + 1) * 512],
                                   start=(kc == 0), stop=(kc == 63))
                return ins
            TM9[gi] = sp.T(dnmms, extra=ext)

            def dosilu(s, gi=gi, oc=oc):
                return s.activation(out=g_t[:, :], in_=pool9[gi % 2][:, :],
                                    func=AF.Silu, scale=cm['sc64dp'][:, oc:oc + 1])
            sp.S(dosilu, extra=[(chT, TM9[gi]), (s_st, out_marks.get(gi - 2, 0))])

            def combine(v, gi=gi):
                v.tensor_tensor(out=ot[gi % 2][:, :], in0=g_t[:, :],
                                in1=s0_t[gi % 2][:, :], op=ALU.mult)
                return v.tensor_tensor(out=ot[gi % 2][:, :], in0=ot[gi % 2][:, :],
                                       in1=x_t[gi % 2][:, :], op=ALU.add)
            EV9[gi] = sp.V(combine)
            sp.dma(kb.sy, s_st, lambda sync, oc=oc, t0=t0, gi=gi: [
                sync.dma_start(out=x1h_r[:, oc, t0:t0 + 512], in_=ot[gi % 2][:, :])],
                extra=[(chE, EV9[gi])])
            out_marks[gi] = s_st.n
    p9.close()

    TOTAL_ST = s_st.n
    def final_wait(sync):
        sync.wait_ge(s_st.h, TOTAL_ST)
    kb.sy(final_wait)
    return kb


BIG = 1.0e30


def build_l1a(dbg=False):
    kb = KB()
    nc = kb.nc
    x1 = kb.din("x1", [D, T_ALL])
    wq1 = kb.din("wq1", [D, 1024])
    wk1 = kb.din("wk1", [D, 1024])
    wv1 = kb.din("wv1", [D, 1024])
    wab = kb.din("wab", [D, 16])
    bab = kb.din("bab", [1, 16])
    lamb_p = kb.din("lamb_p", [128, 8])
    onw = kb.din("onw", [128, 1])
    wo_r = kb.din("wo_r", [1024, D])
    c_onesb = kb.din("c_onesb", [128, 1], BF16)
    c_onesf = kb.din("c_onesf", [128, 1])
    c_onesro = kb.din("c_onesro", [1, 128])
    c_identb = kb.din("c_identb", [128, 128], BF16)
    c_identf = kb.din("c_identf", [128, 128])
    c_ut = kb.din("c_ut", [128, 128])
    c_mask8 = kb.din("c_mask8", [128, 1024])
    s1p = kb.dout("s1p", [D, T_ALL])

    qscr1 = kb.dscr("qscr1", [1024, T_ALL], BF16, dbg)
    kscr1 = kb.dscr("kscr1", [1024, T_ALL], BF16, dbg)
    vscr1 = kb.dscr("vscr1", [1024, T_ALL], BF16, dbg)
    gbscr = kb.dscr("gbscr", [T_ALL, 16], F32, dbg)
    oscr = kb.dscr("oscr", [1024, T_ALL], BF16, dbg)
    if dbg:
        xdscr = kb.dscr("xdscr", [1024, T_ALL], BF16, dbg)
        odscr = kb.dscr("odscr", [1024, T_ALL], F32, dbg)
        mdscr = kb.dscr("mdscr", [128, 4 * 1024], BF16, dbg)
        ddscr = kb.dscr("ddscr", [128, 4 * 1024], F32, dbg)

    x1_r = x1.rearrange("(c p) t -> p c t", p=128)
    qscr1_r = qscr1.rearrange("(h p) t -> p h t", p=128)
    kscr1_r = kscr1.rearrange("(h p) t -> p h t", p=128)
    vscr1_r = vscr1.rearrange("(h p) t -> p h t", p=128)
    oscr_r = oscr.rearrange("(h p) t -> p h t", p=128)
    s1p_r = s1p.rearrange("(c p) t -> p c t", p=128)

    cm = {}
    cm['onesb'] = kb.sb("onesb", [128, 1], BF16)
    cm['onesf'] = kb.sb("onesf", [128, 1], F32)
    cm['onesrow'] = kb.sb("onesrow", [1, 128], F32)
    cm['identb'] = kb.sb("identb", [128, 128], BF16)
    cm['identf'] = kb.sb("identf", [128, 128], F32)
    cm['ut'] = kb.sb("ut", [128, 128], F32)
    cm['mask8'] = kb.sb("mask8", [128, 1024], F32)
    cm['lamb'] = kb.sb("lambS", [128, 8], F32)
    cm['lambe'] = kb.sb("lambeS", [128, 8], F32)
    cm['onw'] = kb.sb("onwS", [128, 1], F32)
    cm['babb'] = kb.sb("babb", [128, 16], F32)
    diagl = kb.sb("diagl", [128, 8 * 128], BF16)

    kb.block = kb.es.enter_context(nc.Block())
    sp = Spine(kb)
    chT = sp.chain('T')
    chE = sp.chain('E')
    s_ldc = kb.sem("ldc")
    s_st = kb.sem("st")
    s_ldg = kb.sem("ldg")
    s_lds = kb.sem("lds")

    CONSTS = sp.dma(kb.sy, s_ldc, lambda sync: [
        sync.dma_start(out=cm['onesb'][:, :], in_=c_onesb),
        sync.dma_start(out=cm['onesf'][:, :], in_=c_onesf),
        sync.dma_start(out=cm['onesrow'][:, :], in_=c_onesro),
        sync.dma_start(out=cm['identb'][:, :], in_=c_identb),
        sync.dma_start(out=cm['identf'][:, :], in_=c_identf),
        sync.dma_start(out=cm['ut'][:, :], in_=c_ut),
        sync.dma_start(out=cm['mask8'][:, :], in_=c_mask8),
        sync.dma_start(out=cm['lamb'][:, :], in_=lamb_p),
        sync.dma_start(out=cm['onw'][:, :], in_=onw),
    ])
    p0 = ExitStack()
    ps00 = kb.ps("ps00", [128, 128], F32, p0)
    t00 = kb.sb("t00", [128, 24], F32, p0)
    babrow = kb.sb("babrow", [1, 16], F32, p0)

    def lprep1(v):
        return v.tensor_scalar(out=t00[:, 8:16], in0=cm['lamb'][:, :], scalar1=0.0,
                               scalar2=None, op0=ALU.max)
    sp.V(lprep1, extra=[(s_ldc, CONSTS)])

    def lprep2(s):
        s.activation(out=t00[:, 0:8], in_=cm['lamb'][:, :], func=AF.Abs)
        return s.activation(out=t00[:, 16:24], in_=t00[:, 0:8], func=AF.Exp,
                            scale=-1.0)
    sp.S(lprep2)

    def lprep3(v):
        return v.tensor_scalar(out=t00[:, 16:24], in0=t00[:, 16:24], scalar1=1.0,
                               scalar2=None, op0=ALU.add)
    sp.V(lprep3)

    def lprep4(s):
        return s.activation(out=t00[:, 0:8], in_=t00[:, 16:24], func=AF.Ln)
    sp.S(lprep4)

    sp.V(lambda v: v.tensor_tensor(out=cm['lamb'][:, :], in0=t00[:, 8:16],
                                   in1=t00[:, 0:8], op=ALU.add))
    sp.V(lambda v: v.tensor_scalar(out=cm['lamb'][:, :], in0=cm['lamb'][:, :],
                                   scalar1=0.25, scalar2=None, op0=ALU.add))
    sp.V(lambda v: v.tensor_scalar(out=cm['lambe'][:, :], in0=cm['lamb'][:, :],
                                   scalar1=1e-8, scalar2=None, op0=ALU.add))

    def ldiag(v):
        ins = None
        for h in range(8):
            ins = v.tensor_scalar(out=diagl[:, h * 128:(h + 1) * 128],
                                  in0=cm['identf'][:, :],
                                  scalar1=cm['lamb'][:, h:h + 1], scalar2=None,
                                  op0=ALU.mult)
        return ins
    sp.V(ldiag)

    BB = sp.dma(kb.sy, s_ldc, lambda sync: [sync.dma_start(out=babrow[:, :], in_=bab)])

    def bbmm(t):
        return t.matmul(ps00[:, 0:16], cm['onesrow'][0:1, :], babrow[0:1, :],
                        start=True, stop=True)
    TM = sp.T(bbmm, extra=[(s_ldc, BB), (chE, chE.n)])
    sp.S(lambda s: s.activation(out=cm['babb'][:, :], in_=ps00[:, 0:16],
                                func=AF.Identity), extra=[(chT, TM)])
    p0.close()

    p1 = ExitStack()
    wq1b = kb.sb("wq1b", [128, 16 * 1024], BF16, p1)
    wk1b = kb.sb("wk1b", [128, 16 * 1024], BF16, p1)
    wv1b = kb.sb("wv1b", [128, 16 * 1024], BF16, p1)
    wabb = kb.sb("wabb", [128, 16 * 16], BF16, p1)
    xnt = kb.sb("xnt", [128, 16 * 512], BF16, p1)
    praw = kb.sb("praw", [128, 8 * 512], BF16, p1)
    psq = kb.sb("psq", [128, 8 * 512], BF16, p1)
    prow = kb.sb("prow", [1, 4 * 512], F32, p1)
    gbt = kb.sb("gbt", [128, 4 * 16], F32, p1)
    gbw = kb.sb("gbw", [128, 6 * 16], F32, p1)

    WLD1 = sp.dma(kb.g, s_ldg, lambda g: [
        g.dma_start(out=wq1b[:, :].rearrange("p (kc m) -> p kc m", kc=16),
                    in_=wq1.rearrange("(kc p) m -> p kc m", p=128)),
        g.dma_start(out=wk1b[:, :].rearrange("p (kc m) -> p kc m", kc=16),
                    in_=wk1.rearrange("(kc p) m -> p kc m", p=128)),
        g.dma_start(out=wv1b[:, :].rearrange("p (kc m) -> p kc m", kc=16),
                    in_=wv1.rearrange("(kc p) m -> p kc m", p=128)),
        g.dma_start(out=wabb[:, :].rearrange("p (kc m) -> p kc m", kc=16),
                    in_=wab.rearrange("(kc p) m -> p kc m", p=128)),
    ], extra=[(chE, chE.n)])

    def xn1_dst(c, t0):
        return xnt[:, c * 512:(c + 1) * 512]

    proj_marks = {'T': 0}

    def pre_xn1(v, i):
        if i >= 1:
            v.wait_ge(chT.h, proj_marks['T'])

    ps_p = kb.ps("ps_p", [128, 2 * 512], F32, p1)
    ps_misc = kb.ps("ps_misc", [128, 512], F32, p1)
    ps_r = ps_misc[0:1, :]
    ps_ab = ps_misc[:, 0:16]
    ps_pb = kb.ps("ps_pb", [128, 512], F32, p1)
    s_ln1 = kb.sem("lnoutA")
    store_marks = {}
    gb_marks = {}

    def post_tile1(i, t0):
        for pi, (wb, scr) in enumerate(((wq1b, qscr1_r), (wk1b, kscr1_r),
                                        (wv1b, vscr1_r))):
            l2 = pi < 2
            for oc in range(8):
                gi = oc % 2

                def pmms(t, wb=wb, oc=oc, gi=gi, i=i):
                    t.wait_ge(s_ln1.h, i + 1)
                    ins = None
                    for kc in range(16):
                        ins = t.matmul(ps_p[:, gi * 512:(gi + 1) * 512],
                                       wb[:, kc * 1024 + oc * 128:kc * 1024 + oc * 128 + 128],
                                       xnt[:, kc * 512:(kc + 1) * 512],
                                       start=(kc == 0), stop=(kc == 15))
                    return ins
                TM = sp.T(pmms, extra=[(s_ldg, WLD1), (chE, chE.n)])

                def pevac(s, oc=oc, gi=gi, l2=l2, pi=pi):
                    s.wait_ge(s_st.h, store_marks.get((pi, oc), 0))
                    ins = s.activation(out=praw[:, oc * 512:(oc + 1) * 512],
                                       in_=ps_p[:, gi * 512:(gi + 1) * 512],
                                       func=AF.Silu)
                    if l2:
                        ins = s.activation(out=psq[:, oc * 512:(oc + 1) * 512],
                                           in_=praw[:, oc * 512:(oc + 1) * 512],
                                           func=AF.Square)
                    return ins
                EV = sp.S(pevac, extra=[(chT, TM)])
                if l2:
                    def nrm_mm(t, oc=oc):
                        return t.matmul(ps_r, cm['onesb'][:, :],
                                        psq[:, oc * 512:(oc + 1) * 512],
                                        start=True, stop=True)
                    TM = sp.T(nrm_mm, extra=[(chE, EV)])

                    def nrm_row(v, oc=oc):
                        c0 = (oc % 4) * 512
                        return v.tensor_scalar(out=prow[0:1, c0:c0 + 512],
                                               in0=ps_r, scalar1=0.0,
                                               scalar2=None, op0=ALU.add)
                    sp.V(nrm_row, extra=[(chT, TM)])

                    def nrm_sqrt(s, oc=oc):
                        c0 = (oc % 4) * 512
                        return s.activation(out=prow[0:1, c0:c0 + 512],
                                            in_=prow[0:1, c0:c0 + 512], func=AF.Sqrt)
                    sp.S(nrm_sqrt)

                    def nrm_fin(v, oc=oc):
                        c0 = (oc % 4) * 512
                        v.tensor_scalar(out=prow[0:1, c0:c0 + 512],
                                        in0=prow[0:1, c0:c0 + 512], scalar1=1e-12,
                                        scalar2=None, op0=ALU.max)
                        return v.reciprocal(out=prow[0:1, c0:c0 + 512],
                                            in_=prow[0:1, c0:c0 + 512])
                    EV = sp.V(nrm_fin)

                    def nrm_bc(t, oc=oc):
                        c0 = (oc % 4) * 512
                        return t.matmul(ps_pb[:, :], cm['onesrow'][0:1, :],
                                        prow[0:1, c0:c0 + 512], start=True, stop=True)
                    TM = sp.T(nrm_bc, extra=[(chE, EV)])

                    def nrm_mul(v, oc=oc):
                        return v.tensor_tensor(out=praw[:, oc * 512:(oc + 1) * 512],
                                               in0=praw[:, oc * 512:(oc + 1) * 512],
                                               in1=ps_pb[:, :], op=ALU.mult)
                    EV = sp.V(nrm_mul, extra=[(chT, TM)])
                    src_t = praw
                else:
                    src_t = praw
                sp.dma(kb.sy, s_st, lambda sync, scr=scr, oc=oc, t0=t0, src_t=src_t: [
                    sync.dma_start(out=scr[:, oc, t0:t0 + 512],
                                   in_=src_t[:, oc * 512:(oc + 1) * 512])],
                    extra=[(chE, EV)])
                store_marks[(pi, oc)] = s_st.n

        for stt in range(4):
            def abmms(t, stt=stt):
                ins = None
                for kc in range(16):
                    ins = t.matmul(ps_ab,
                                   xnt[:, kc * 512 + stt * 128:kc * 512 + stt * 128 + 128],
                                   wabb[:, kc * 16:(kc + 1) * 16],
                                   start=(kc == 0), stop=(kc == 15))
                return ins
            TM = sp.T(abmms, extra=[(chE, chE.n)])

            def gbmath(v, stt=stt):
                v.wait_ge(s_st.h, gb_marks.get(stt, 0))
                return v.tensor_tensor(out=gbw[:, 0:16], in0=ps_ab,
                                       in1=cm['babb'][:, :], op=ALU.add)
            sp.V(gbmath, extra=[(chT, TM)])
            EV = sp.V(lambda v: v.tensor_scalar(out=gbw[:, 24:32], in0=gbw[:, 0:8],
                                                scalar1=0.0, scalar2=None,
                                                op0=ALU.min))
            def gbact(s, stt=stt):
                s.activation(out=gbw[:, 16:24], in_=gbw[:, 0:8], func=AF.Abs)
                s.activation(out=gbw[:, 32:40], in_=gbw[:, 16:24], func=AF.Exp,
                             scale=-1.0)
                return s.activation(out=gbw[:, 48:56], in_=gbw[:, 8:16], func=AF.Sigmoid)
            sp.S(gbact)

            def gb2(v, stt=stt):
                return v.tensor_scalar(out=gbw[:, 32:40], in0=gbw[:, 32:40],
                                       scalar1=1.0, scalar2=None, op0=ALU.add)
            sp.V(gb2)

            def gb3(s, stt=stt):
                return s.activation(out=gbw[:, 40:48], in_=gbw[:, 32:40], func=AF.Ln)
            sp.S(gb3)

            sp.V(lambda v, stt=stt: v.tensor_tensor(
                out=gbt[:, stt * 16:stt * 16 + 8], in0=gbw[:, 24:32],
                in1=gbw[:, 40:48], op=ALU.subtract))
            EV = sp.V(lambda v, stt=stt: v.tensor_copy(
                out=gbt[:, stt * 16 + 8:stt * 16 + 16], in_=gbw[:, 48:56]))
            sp.dma(kb.sy, s_st, lambda sync, stt=stt, t0=t0: [sync.dma_start(
                out=gbscr[t0 + stt * 128:t0 + stt * 128 + 128, :],
                in_=gbt[:, stt * 16:(stt + 1) * 16])], extra=[(chE, EV)])
            gb_marks[stt] = s_st.n
        proj_marks['T'] = chT.n

    emit_ln(kb, "A", x1_r, [i * 512 for i in range(8)], xn1_dst, cm, p1,
                    pre_xn=pre_xn1, post_tile=post_tile1, gate=[(chE, 1)])
    P1A_ST = s_st.n
    p1.close()

    p2 = ExitStack()
    kc_fm = kb.sb("kc_fm", [128, 1024], BF16, p2)
    qc_fm = kb.sb("qc_fm", [128, 1024], BF16, p2)
    vc_fm = kb.sb("vc_fm", [128, 1024], BF16, p2)
    k_tm = kb.sb("k_tm", [128, 1024], BF16, p2)
    v_tm = kb.sb("v_tm", [128, 1024], BF16, p2)
    k2_tm = kb.sb("k2_tm", [128, 1024], BF16, p2)
    Mt = kb.sb("Mt", [128, 1024], BF16, p2)
    ebc = kb.sb("ebc", [128, 1024], BF16, p2)
    Hkk = kb.sb("Hkk", [128, 1024], F32, p2)
    HkvT = kb.sb("HkvT", [128, 1024], F32, p2)
    Hb = kb.sb("Hb", [128, 1024], BF16, p2)
    HkvTb = kb.sb("HkvTb", [128, 1024], BF16, p2)
    d0col = kb.sb("d0col", [128, 8], F32, p2)
    d0rowb = kb.sb("d0rowb", [1, 1024], BF16, p2)
    cg_td = kb.sb("cg_td", [128, 8], F32, p2)
    ecgrowb = kb.sb("ecgrowb", [1, 1024], BF16, p2)
    bias8 = kb.sb("bias8", [128, 8], F32, p2)
    lnb8 = kb.sb("lnb8", [128, 8], F32, p2)
    Xv = kb.sb("Xv", [128, 1024], BF16, p2)
    Rv = kb.sb("Rv", [128, 1024], BF16, p2)
    Pb = kb.sb("Pb", [128, 1024], BF16, p2)
    Ptb = kb.sb("Ptb", [128, 1024], BF16, p2)
    Zb = kb.sb("Zb", [128, 1024], BF16, p2)
    qf = kb.sb("qf", [128, 1024], BF16, p2)
    Sb = kb.sb("Sb", [128, 1024], BF16, p2)
    prod = kb.sb("prod", [128, 1024], BF16, p2)
    tmpa = kb.sb("tmpa", [128, 1024], BF16, p2)
    tmpb = kb.sb("tmpb", [128, 1024], BF16, p2)
    den = kb.sb("den", [128, 1024], F32, p2)
    s8 = kb.sb("s8", [128, 64], F32, p2)
    arow = kb.sb("arow", [1, 1024], F32, p2)
    gb8 = kb.sb("gb8", [128, 16], F32, p2)
    wc8 = kb.sb("wc8", [128, 8], F32, p2)
    deccol = kb.sb("deccol", [128, 8], F32, p2)
    kw_tm = kb.sb("kw_tm", [128, 1024], BF16, p2)
    vw_tm = kb.sb("vw_tm", [128, 1024], BF16, p2)
    k2w_tm = kb.sb("k2w_tm", [128, 1024], BF16, p2)
    of = kb.sb("of", [128, 1024], F32, p2)
    on_t = kb.sb("on_t", [128, 1024], BF16, p2)
    ps_S = kb.ps("psA_S", [128, 1024], F32, p2)
    ps_q = kb.ps("psA_q", [128, 1024], F32, p2)
    ps_b = kb.ps("psA_b", [128, 1024], F32, p2)
    ps_aux = kb.ps("psA_x", [128, 512], F32, p2)
    ps_trb = ps_S[:, 0:512].bitcast(BF16)
    s_mld = kb.sem("mld")

    def minit(v):
        v.memset(Hkk[:, :], 0.0)
        v.memset(HkvT[:, :], 0.0)
        v.memset(Hb[:, :], 0.0)
        v.memset(HkvTb[:, :], 0.0)
        v.memset(d0col[:, :], 0.0)
        return v.memset(d0rowb[:, :], 0.0)
    sp.V(minit, extra=[(chT, chT.n)])

    def mesa_mv(src_b, src_tilde, dst_extra):
        def smms(t):
            ins = None
            for h in range(8):
                ins = t.matmul(ps_S[:, h * 128:(h + 1) * 128],
                               kc_fm[:, h * 128:(h + 1) * 128],
                               src_b[:, h * 128:(h + 1) * 128],
                               start=True, stop=True)
            return ins
        TM = sp.T(smms, extra=dst_extra)

        def zev(s):
            return s.activation(out=Sb[:, :], in_=ps_S[:, :], func=AF.Identity)
        sp.S(zev, extra=[(chT, TM)])

        def zmul(v):
            return v.tensor_tensor(out=Zb[:, :], in0=Mt[:, :], in1=Sb[:, :],
                                   op=ALU.mult)
        EV = sp.V(zmul)

        def qmms(t):
            ins = None
            for h in range(8):
                sl = slice(h * 128, (h + 1) * 128)
                t.matmul(ps_q[:, sl], Hb[:, sl], src_tilde[:, sl],
                         start=True, stop=False)
                t.matmul(ps_q[:, sl], diagl[:, sl], src_b[:, sl],
                         start=False, stop=False)
                ins = t.matmul(ps_q[:, sl], k_tm[:, sl], Zb[:, sl],
                               start=False, stop=True)
            return ins
        return sp.T(qmms, extra=[(chE, EV)])

    def dots8(src, cols, extra):
        def dmms(t):
            ins = None
            for h in range(8):
                ins = t.matmul(ps_aux[:, cols + h:cols + h + 1],
                               src[:, h * 128:(h + 1) * 128], cm['onesb'][:, :],
                               start=True, stop=True)
            return ins
        return sp.T(dmms, extra=extra)

    def rowbc(col_ap, extra):
        def rmms(t):
            ins = None
            for h in range(8):
                ins = t.matmul(ps_S[0:1, h * 128:(h + 1) * 128],
                               col_ap[:, h:h + 1], cm['identf'][:, :],
                               start=True, stop=True)
            return ins
        TM = sp.T(rmms, extra=extra)
        EV = sp.S(lambda s: s.activation(out=arow[:, :], in_=ps_S[0:1, 0:1024],
                                         func=AF.Identity), extra=[(chT, TM)])

        def bmms(t):
            t.matmul(ps_b[:, 0:512], cm['onesrow'][0:1, :], arow[0:1, 0:512],
                     start=True, stop=True)
            return t.matmul(ps_b[:, 512:1024], cm['onesrow'][0:1, :],
                            arow[0:1, 512:1024], start=True, stop=True)
        return sp.T(bmms, extra=[(chE, EV)])

    for ch in range(32):
        t0 = ch * 128
        MLD = sp.dma(kb.sy, s_mld, lambda sync, t0=t0: [
            sync.dma_start(out=kc_fm[:, :].rearrange("p (h t) -> p h t", h=8),
                           in_=kscr1_r[:, :, t0:t0 + 128]),
            sync.dma_start(out=qc_fm[:, :].rearrange("p (h t) -> p h t", h=8),
                           in_=qscr1_r[:, :, t0:t0 + 128]),
            sync.dma_start(out=vc_fm[:, :].rearrange("p (h t) -> p h t", h=8),
                           in_=vscr1_r[:, :, t0:t0 + 128]),
            sync.dma_start(out=gb8[:, :], in_=gbscr[t0:t0 + 128, :])],
            extra=[(s_st, P1A_ST if ch == 0 else 0), (chT, chT.n), (chE, chE.n)])

        def trkv(t):
            ins = None
            for h in range(8):
                t.transpose(ps_trb[:, h * 128:(h + 1) * 128],
                            kc_fm[:, h * 128:(h + 1) * 128], cm['identb'][:, :])
            for h in range(8):
                ins = t.transpose(ps_S[:, 512:1024].bitcast(BF16)[:, h * 128:(h + 1) * 128],
                                  vc_fm[:, h * 128:(h + 1) * 128], cm['identb'][:, :])
            return ins
        TM = sp.T(trkv, extra=[(s_mld, MLD), (chE, chE.n)])

        def kvev(s):
            s.activation(out=k_tm[:, :], in_=ps_trb[:, :], func=AF.Identity)
            s.activation(out=v_tm[:, :], in_=ps_S[:, 512:1024].bitcast(BF16),
                         func=AF.Identity)
            return s.activation(out=k2_tm[:, :], in_=k_tm[:, :], func=AF.Square)
        sp.S(kvev, extra=[(chT, TM)])

        def cgcol(t):
            return t.matmul(ps_aux[:, 24:32], cm['ut'][:, :], gb8[:, 0:8],
                            start=True, stop=True)
        TM2 = sp.T(cgcol, extra=[(chE, chE.n)])

        def cgcev(v):
            return v.tensor_copy(out=cg_td[:, :], in_=ps_aux[:, 24:32])
        sp.V(cgcev, extra=[(chT, TM2)])

        def lnbeta(s):
            return s.activation(out=lnb8[:, :], in_=gb8[:, 8:16], func=AF.Ln)
        sp.S(lnbeta)

        def biasc(v):
            return v.tensor_tensor(out=bias8[:, :], in0=lnb8[:, :], in1=cg_td[:, :],
                                   op=ALU.subtract)
        EVb = sp.V(biasc)

        def cgrowx(t):
            ins = None
            for h in range(8):
                ins = t.matmul(ps_S[0:1, h * 128:(h + 1) * 128],
                               cg_td[:, h:h + 1], cm['identf'][:, :],
                               start=True, stop=True)
            return ins
        TM = sp.T(cgrowx, extra=[(chE, chE.n)])
        EV = sp.S(lambda s: s.activation(out=arow[:, :], in_=ps_S[0:1, 0:1024],
                                         func=AF.Identity), extra=[(chT, TM)])
        sp.S(lambda s: s.activation(out=ecgrowb[:, :], in_=arow[:, :], func=AF.Exp))

        def mpre(t):
            t.matmul(ps_b[:, 0:512], cm['onesrow'][0:1, :], arow[0:1, 0:512],
                     start=True, stop=True)
            return t.matmul(ps_b[:, 512:1024], cm['onesrow'][0:1, :],
                            arow[0:1, 512:1024], start=True, stop=True)
        TM = sp.T(mpre, extra=[(chE, chE.n)])

        def mexp1(s):
            return s.activation(out=ebc[:, :], in_=ps_b[:, :], func=AF.Exp)
        sp.S(mexp1, extra=[(chT, TM)])

        def mmask(v):
            return v.tensor_tensor(out=den[:, :], in0=ps_b[:, :],
                                   in1=cm['mask8'][:, :], op=ALU.add)
        sp.V(mmask)

        def mexp2(s):
            ins = None
            for h in range(8):
                ins = s.activation(out=Mt[:, h * 128:(h + 1) * 128],
                                   in_=den[:, h * 128:(h + 1) * 128], func=AF.Exp,
                                   bias=bias8[:, h:h + 1])
            return ins
        sp.S(mexp2, extra=[(chE, EVb)])

        def dgmm(t):
            ins = None
            for h in range(8):
                sl = slice(h * 128, (h + 1) * 128)
                t.matmul(ps_q[:, sl], k2_tm[:, sl], Mt[:, sl], start=True, stop=False)
                ins = t.matmul(ps_q[:, sl], d0rowb[0:1, sl], ecgrowb[0:1, sl],
                               start=False, stop=True)
            return ins
        TM = sp.T(dgmm, extra=[(chE, chE.n)])

        def denev(s):
            ins = None
            for h in range(8):
                ins = s.activation(out=den[:, h * 128:(h + 1) * 128],
                                   in_=ps_q[:, h * 128:(h + 1) * 128],
                                   func=AF.Identity, bias=cm['lambe'][:, h:h + 1])
            return ins
        sp.S(denev, extra=[(chT, TM)])

        def x0c(v):
            v.reciprocal(out=den[:, :], in_=den[:, :])
            v.tensor_tensor(out=Xv[:, :], in0=qc_fm[:, :], in1=den[:, :], op=ALU.mult)
            return v.tensor_tensor(out=Ptb[:, :], in0=Xv[:, :], in1=ebc[:, :],
                                   op=ALU.mult)
        sp.V(x0c)

        TMq = mesa_mv(Xv, Ptb, [(chE, chE.n)])

        def r0c(s):
            return s.activation(out=qf[:, :], in_=ps_q[:, :], func=AF.Identity)
        sp.S(r0c, extra=[(chT, TMq)])

        def r0v(v):
            v.tensor_tensor(out=Rv[:, :], in0=qc_fm[:, :], in1=qf[:, :],
                            op=ALU.subtract)
            v.tensor_copy(out=Pb[:, :], in_=Rv[:, :])
            v.memset(s8[:, 24:32], 0.0)
            return v.tensor_tensor(out=prod[:, :], in0=Rv[:, :], in1=Rv[:, :],
                                   op=ALU.mult)
        EV = sp.V(r0v)
        dots8(prod, 0, [(chE, EV)])
        def ddev(v):
            return v.tensor_copy(out=s8[:, 0:8], in_=ps_aux[:, 0:8])
        sp.V(ddev, extra=[(chT, chT.n)])

        for it in range(30):
            sp.V(lambda v: v.tensor_scalar(out=s8[:, 32:40], in0=s8[:, 0:8],
                                           scalar1=1e-10, scalar2=None,
                                           op0=ALU.is_lt))
            def itA(v):
                v.tensor_tensor(out=s8[:, 24:32], in0=s8[:, 24:32], in1=s8[:, 32:40],
                                op=ALU.max)
                return v.tensor_tensor(out=Ptb[:, :], in0=Pb[:, :], in1=ebc[:, :],
                                       op=ALU.mult)
            sp.V(itA)

            TMq = mesa_mv(Pb, Ptb, [(chE, chE.n)])

            def itE(s):
                return s.activation(out=qf[:, :], in_=ps_q[:, :], func=AF.Identity)
            sp.S(itE, extra=[(chT, TMq)])

            def itF(v):
                return v.tensor_tensor(out=prod[:, :], in0=Pb[:, :], in1=qf[:, :],
                                       op=ALU.mult)
            EV = sp.V(itF)
            dots8(prod, 8, [(chE, EV)])

            sp.V(lambda v: v.tensor_copy(out=s8[:, 16:24], in_=ps_aux[:, 8:16]),
                 extra=[(chT, chT.n)])
            sp.V(lambda v: v.tensor_tensor(out=s8[:, 16:24], in0=s8[:, 16:24],
                                           in1=s8[:, 16:24], op=ALU.mult))
            sp.V(lambda v: v.tensor_scalar(out=s8[:, 32:40], in0=s8[:, 16:24],
                                           scalar1=1e-20, scalar2=None,
                                           op0=ALU.is_lt))
            sp.V(lambda v: v.tensor_tensor(out=s8[:, 24:32], in0=s8[:, 24:32],
                                           in1=s8[:, 32:40], op=ALU.max))
            sp.V(lambda v: v.scalar_tensor_tensor(out=s8[:, 32:40], in0=s8[:, 24:32],
                                                  scalar=BIG, in1=ps_aux[:, 8:16],
                                                  op0=ALU.mult, op1=ALU.add))
            sp.V(lambda v: v.reciprocal(out=s8[:, 40:48], in_=s8[:, 32:40]))
            EV = sp.V(lambda v: v.tensor_tensor(out=s8[:, 48:56], in0=s8[:, 0:8],
                                                in1=s8[:, 40:48], op=ALU.mult))

            rowbc(s8[:, 48:56], [(chE, EV)])

            def itL(v):
                v.tensor_tensor(out=tmpa[:, :], in0=Pb[:, :], in1=ps_b[:, :],
                                op=ALU.mult)
                v.tensor_tensor(out=Xv[:, :], in0=Xv[:, :], in1=tmpa[:, :],
                                op=ALU.add)
                v.tensor_tensor(out=tmpb[:, :], in0=qf[:, :], in1=ps_b[:, :],
                                op=ALU.mult)
                v.tensor_tensor(out=Rv[:, :], in0=Rv[:, :], in1=tmpb[:, :],
                                op=ALU.subtract)
                return v.tensor_tensor(out=prod[:, :], in0=Rv[:, :], in1=Rv[:, :],
                                       op=ALU.mult)
            EV = sp.V(itL, extra=[(chT, chT.n)])
            dots8(prod, 0, [(chE, EV)])

            sp.V(lambda v: v.scalar_tensor_tensor(out=s8[:, 32:40], in0=s8[:, 24:32],
                                                  scalar=BIG, in1=s8[:, 0:8],
                                                  op0=ALU.mult, op1=ALU.add),
                 extra=[(chT, chT.n)])
            sp.V(lambda v: v.reciprocal(out=s8[:, 40:48], in_=s8[:, 32:40]))
            sp.V(lambda v: v.tensor_tensor(out=s8[:, 56:64], in0=ps_aux[:, 0:8],
                                           in1=s8[:, 40:48], op=ALU.mult))
            EV = sp.V(lambda v: v.tensor_copy(out=s8[:, 0:8], in_=ps_aux[:, 0:8]))

            rowbc(s8[:, 56:64], [(chE, EV)])

            def itS(v):
                v.tensor_tensor(out=tmpa[:, :], in0=Pb[:, :], in1=ps_b[:, :],
                                op=ALU.mult)
                return v.tensor_tensor(out=Pb[:, :], in0=Rv[:, :], in1=tmpa[:, :],
                                       op=ALU.add)
            sp.V(itS, extra=[(chT, chT.n)])

        if dbg:
            def dbgst(sync, ch=ch, t0=t0):
                s_st.inc(sync.dma_start(
                    out=xdscr.rearrange("(h p) t -> p h t", p=128)[:, :, t0:t0 + 128],
                    in_=Xv[:, :].rearrange("p (h t) -> p h t", h=8)), 16)
                if ch < 4:
                    s_st.inc(sync.dma_start(out=mdscr[:, ch * 1024:(ch + 1) * 1024],
                                            in_=Mt[:, :]), 16)
                    s_st.inc(sync.dma_start(out=ddscr[:, ch * 1024:(ch + 1) * 1024],
                                            in_=den[:, :]), 16)
            def dbgwait(v):
                return v.engine_nop()
            EVd = sp.V(dbgwait)
            kb.sy(lambda sync, EVd=EVd: (sync.wait_ge(chE.h, EVd), dbgst(sync))[-1])

        def postA(v):
            return v.tensor_tensor(out=Ptb[:, :], in0=Xv[:, :], in1=ebc[:, :],
                                   op=ALU.mult)
        sp.V(postA)

        def osm(t):
            ins = None
            for h in range(8):
                sl = slice(h * 128, (h + 1) * 128)
                ins = t.matmul(ps_S[:, sl], vc_fm[:, sl], Xv[:, sl],
                               start=True, stop=True)
            return ins
        TM = sp.T(osm, extra=[(chE, chE.n)])

        def ozev(s):
            return s.activation(out=Sb[:, :], in_=ps_S[:, :], func=AF.Identity)
        sp.S(ozev, extra=[(chT, TM)])

        def ozm(v):
            return v.tensor_tensor(out=Zb[:, :], in0=Mt[:, :], in1=Sb[:, :],
                                   op=ALU.mult)
        EV = sp.V(ozm)

        def omms(t):
            ins = None
            for h in range(8):
                sl = slice(h * 128, (h + 1) * 128)
                t.matmul(ps_q[:, sl], HkvTb[:, sl], Ptb[:, sl], start=True, stop=False)
                ins = t.matmul(ps_q[:, sl], k_tm[:, sl], Zb[:, sl],
                               start=False, stop=True)
            return ins
        TM = sp.T(omms, extra=[(chE, EV)])

        def oev(s):
            ins = s.activation(out=of[:, :], in_=ps_q[:, :], func=AF.Identity)
            return s.activation(out=prod[:, :], in_=of[:, :], func=AF.Square)
        EV = sp.S(oev, extra=[(chT, TM)])
        if dbg:
            sp.dma(kb.sy, s_st, lambda sync, t0=t0: [sync.dma_start(
                out=odscr.rearrange("(h p) t -> p h t", p=128)[:, :, t0:t0 + 128],
                in_=of[:, :].rearrange("p (h t) -> p h t", h=8))],
                extra=[(chE, EV)])
        dots8(prod, 0, [(chE, EV)])

        def orms(v):
            v.tensor_scalar(out=s8[:, 32:40], in0=ps_aux[:, 0:8],
                            scalar1=1.0 / 128, scalar2=1e-6, op0=ALU.mult, op1=ALU.add)
            return v.engine_nop()
        sp.V(orms, extra=[(chT, chT.n)])
        sp.S(lambda s: s.activation(out=s8[:, 40:48], in_=s8[:, 32:40], func=AF.Sqrt))
        def orr(v):
            return v.reciprocal(out=s8[:, 48:56], in_=s8[:, 40:48])
        EV = sp.V(orr)
        rowbc(s8[:, 48:56], [(chE, EV)])

        STP = s_st.n
        def onorm(v):
            v.tensor_tensor(out=of[:, :], in0=of[:, :], in1=ps_b[:, :], op=ALU.mult)
            return v.tensor_scalar(out=on_t[:, :], in0=of[:, :],
                                   scalar1=cm['onw'][:, 0:1], scalar2=None,
                                   op0=ALU.mult)
        EV = sp.V(onorm, extra=[(chT, chT.n), (s_st, STP)])
        sp.dma(kb.sy, s_st, lambda sync, t0=t0: [sync.dma_start(
            out=oscr_r[:, :, t0:t0 + 128],
            in_=on_t[:, :].rearrange("p (h t) -> p h t", h=8))], extra=[(chE, EV)])

        def cgrow2(t):
            ins = None
            for h in range(8):
                ins = t.matmul(ps_S[0:1, h * 128:(h + 1) * 128],
                               cg_td[:, h:h + 1], cm['identf'][:, :],
                               start=True, stop=True)
            return ins
        TM2 = sp.T(cgrow2, extra=[(chE, chE.n)])
        EV = sp.S(lambda s: s.activation(out=arow[:, :], in_=ps_S[0:1, 0:1024],
                                         func=AF.Identity), extra=[(chT, TM2)])

        def wprep2(t):
            ins = None
            for h in range(8):
                ins = t.matmul(ps_aux[:, 40 + h:40 + h + 1], cm['onesrow'][0:1, :],
                               arow[0:1, h * 128 + 127:h * 128 + 128],
                               start=True, stop=True)
            return ins
        TM = sp.T(wprep2, extra=[(chE, EV)])

        def wcomp(v):
            return v.tensor_tensor(out=wc8[:, :], in0=ps_aux[:, 40:48],
                                   in1=cg_td[:, :], op=ALU.subtract)
        sp.V(wcomp, extra=[(chT, TM)])

        def wexp(s):
            s.activation(out=wc8[:, :], in_=wc8[:, :], func=AF.Exp)
            return s.activation(out=deccol[:, :], in_=ps_aux[:, 40:48], func=AF.Exp)
        sp.S(wexp)

        sp.V(lambda v: v.tensor_tensor(out=wc8[:, :], in0=wc8[:, :],
                                       in1=gb8[:, 8:16], op=ALU.mult))

        def wmul(v):
            ins = None
            for h in range(8):
                sl = slice(h * 128, (h + 1) * 128)
                v.tensor_scalar(out=kw_tm[:, sl], in0=k_tm[:, sl],
                                scalar1=wc8[:, h:h + 1], scalar2=None, op0=ALU.mult)
                v.tensor_scalar(out=vw_tm[:, sl], in0=v_tm[:, sl],
                                scalar1=wc8[:, h:h + 1], scalar2=None, op0=ALU.mult)
                ins = v.tensor_scalar(out=k2w_tm[:, sl], in0=k2_tm[:, sl],
                                      scalar1=wc8[:, h:h + 1], scalar2=None,
                                      op0=ALU.mult)
            return ins
        EV = sp.V(wmul)

        def hmms(t):
            ins = None
            for h in range(8):
                sl = slice(h * 128, (h + 1) * 128)
                t.matmul(ps_S[:, sl], kw_tm[:, sl], k_tm[:, sl], start=True, stop=True)
                t.matmul(ps_q[:, sl], vw_tm[:, sl], k_tm[:, sl], start=True, stop=True)
                ins = t.matmul(ps_aux[:, 48 + h:48 + h + 1], k2w_tm[:, sl],
                               cm['onesb'][:, :], start=True, stop=True)
            return ins
        TM = sp.T(hmms, extra=[(chE, EV)])

        def hupd(v):
            ins = None
            for h in range(8):
                sl = slice(h * 128, (h + 1) * 128)
                v.scalar_tensor_tensor(out=Hkk[:, sl], in0=Hkk[:, sl],
                                       scalar=deccol[:, h:h + 1], in1=ps_S[:, sl],
                                       op0=ALU.mult, op1=ALU.add)
                v.scalar_tensor_tensor(out=HkvT[:, sl], in0=HkvT[:, sl],
                                       scalar=deccol[:, h:h + 1], in1=ps_q[:, sl],
                                       op0=ALU.mult, op1=ALU.add)
                ins = v.scalar_tensor_tensor(out=d0col[:, h:h + 1], in0=d0col[:, h:h + 1],
                                             scalar=deccol[:, h:h + 1],
                                             in1=ps_aux[:, 48 + h:48 + h + 1],
                                             op0=ALU.mult, op1=ALU.add)
            return ins
        EV = sp.V(hupd, extra=[(chT, TM)])

        def hcast(s):
            s.activation(out=Hb[:, :], in_=Hkk[:, :], func=AF.Identity)
            return s.activation(out=HkvTb[:, :], in_=HkvT[:, :], func=AF.Identity)
        sp.S(hcast)

        def d0mm(t):
            ins = None
            for h in range(8):
                ins = t.matmul(ps_S[0:1, h * 128:(h + 1) * 128],
                               d0col[:, h:h + 1], cm['identf'][:, :],
                               start=True, stop=True)
            return ins
        TM = sp.T(d0mm, extra=[(chE, chE.n)])
        sp.S(lambda s: s.activation(out=d0rowb[:, :], in_=ps_S[0:1, 0:1024],
                                    func=AF.Identity), extra=[(chT, TM)])
    P2A_ST = s_st.n
    p2.close()

    p3 = ExitStack()
    wo_b = kb.sb("wo_b", [128, 8 * 2048], BF16, p3)
    otl = kb.sb("otl", [128, 8 * 512], BF16, p3)
    so1 = [kb.sb(f"so1{j}", [128, 512], F32, p3) for j in range(2)]
    pso1 = [kb.ps(f"pso1{j}", [128, 512], F32, p3) for j in range(2)]

    WOL = sp.dma(kb.g, s_ldg, lambda g: [g.dma_start(
        out=wo_b[:, :].rearrange("p (kc m) -> p kc m", kc=8),
        in_=wo_r.rearrange("(kc p) m -> p kc m", p=128))],
        extra=[(chE, chE.n), (chT, chT.n)])

    st3_marks = {}
    gi3 = 0
    TM3, EV3 = {}, {}
    for tt in range(8):
        t0 = tt * 512
        OTL = sp.dma(kb.sy, s_lds, lambda sync, t0=t0: [sync.dma_start(
            out=otl[:, :].rearrange("p (h t) -> p h t", h=8),
            in_=oscr_r[:, :, t0:t0 + 512])],
            extra=[(s_st, P2A_ST), (chT, chT.n)])
        for oc in range(16):
            gi = gi3; gi3 += 1

            def womms(t, oc=oc, gi=gi):
                ins = None
                for kc in range(8):
                    ins = t.matmul(pso1[gi % 2][:, :],
                                   wo_b[:, kc * 2048 + oc * 128:kc * 2048 + oc * 128 + 128],
                                   otl[:, kc * 512:(kc + 1) * 512],
                                   start=(kc == 0), stop=(kc == 7))
                return ins
            ext = [(s_ldg, WOL), (s_lds, OTL)]
            ext.append((chE, EV3[gi - 2] if gi >= 2 else chE.n))
            TM3[gi] = sp.T(womms, extra=ext)

            def soev(s, gi=gi):
                return s.activation(out=so1[gi % 2][:, :], in_=pso1[gi % 2][:, :],
                                    func=AF.Identity)
            EV3[gi] = sp.S(soev, extra=[(chT, TM3[gi]), (s_st, st3_marks.get(gi - 2, 0))])
            sp.dma(kb.sy, s_st, lambda sync, oc=oc, t0=t0, gi=gi: [sync.dma_start(
                out=s1p_r[:, oc, t0:t0 + 512], in_=so1[gi % 2][:, :])],
                extra=[(chE, EV3[gi])])
            st3_marks[gi] = s_st.n
    p3.close()

    TOTAL_ST = s_st.n
    def final_wait(sync):
        sync.wait_ge(s_st.h, TOTAL_ST)
    kb.sy(final_wait)
    return kb


def build_l1c(dbg=False):
    kb = KB()
    nc = kb.nc
    x1own = kb.din("x1own", [D, T_OWN])
    s1a = kb.din("s1a", [D, T_OWN])
    s1b = kb.din("s1b", [D, T_OWN])
    router_wT = kb.din("router_wT", [D, 8])
    shgT = kb.din("shgT", [D, 1408])
    shuT = kb.din("shuT", [D, 1408])
    shdT = kb.din("shdT", [1408, D])
    wgTs = kb.din("wgTs", [8 * D, 1408])
    wuTs = kb.din("wuTs", [8 * D, 1408])
    wdTs = kb.din("wdTs", [8 * 1408, D])
    c_onesb = kb.din("c_onesb", [128, 1], BF16)
    c_onesf = kb.din("c_onesf", [128, 1])
    c_onesro = kb.din("c_onesro", [1, 128])
    c_identb = kb.din("c_identb", [128, 128], BF16)
    c_identf = kb.din("c_identf", [128, 128])
    x2h = kb.dout("x2h", [D, T_OWN])
    accscr = kb.dscr("accscr", [D, T_OWN], F32, dbg)
    combscr = kb.dscr("combscr", [16, 1024], F32, dbg)

    x1_r = x1own.rearrange("(c p) t -> p c t", p=128)
    s1a_r = s1a.rearrange("(c p) t -> p c t", p=128)
    s1b_r = s1b.rearrange("(c p) t -> p c t", p=128)
    acc_r = accscr.rearrange("(c p) t -> p c t", p=128)
    x2h_r = x2h.rearrange("(c p) t -> p c t", p=128)

    cm = {}
    cm['onesb'] = kb.sb("onesb", [128, 1], BF16)
    cm['onesf'] = kb.sb("onesf", [128, 1], F32)
    cm['onesrow'] = kb.sb("onesrow", [1, 128], F32)
    cm['identb'] = kb.sb("identb", [128, 128], BF16)
    cm['identf'] = kb.sb("identf", [128, 128], F32)
    rwb = kb.sb("rwb", [128, 16 * 8], BF16)
    xn_own = kb.sb("xnC", [128, NCH * T_OWN], BF16)

    kb.block = kb.es.enter_context(nc.Block())
    sp = Spine(kb)
    chT = sp.chain('T')
    chE = sp.chain('E')
    s_ldc = kb.sem("ldc")
    s_st = kb.sem("st")
    s_ldg = kb.sem("ldg")
    s_lds = kb.sem("lds")
    s_acc = kb.sem("acc")

    CONSTS = sp.dma(kb.sy, s_ldc, lambda sync: [
        sync.dma_start(out=cm['onesb'][:, :], in_=c_onesb),
        sync.dma_start(out=cm['onesf'][:, :], in_=c_onesf),
        sync.dma_start(out=cm['onesrow'][:, :], in_=c_onesro),
        sync.dma_start(out=cm['identb'][:, :], in_=c_identb),
        sync.dma_start(out=cm['identf'][:, :], in_=c_identf),
    ])
    RWL = sp.dma(kb.g, s_ldg, lambda g: [g.dma_start(
        out=rwb[:, :].rearrange("p (kc m) -> p kc m", kc=16),
        in_=router_wT.rearrange("(kc p) m -> p kc m", p=128))])

    def nopv(v):
        return v.engine_nop()
    sp.V(nopv, extra=[(s_ldc, CONSTS), (s_ldg, RWL)])

    p1 = ExitStack()

    def xn_dst(c, t0):
        return xn_own[:, c * T_OWN + t0:c * T_OWN + t0 + 512]

    s_ln = emit_ln(kb, "C", x1_r, [i * 512 for i in range(4)], xn_dst, cm, p1,
                   gate=[(chE, 1)])

    def c1done(v):
        v.wait_ge(s_ln.h, 4)
        return v.engine_nop()
    sp.V(c1done)
    p1.close()

    p2 = ExitStack()
    r8 = kb.sb("r8", [128, 64], F32, p2)
    crowt = kb.sb("crowt", [1, 512], F32, p2)
    ps_r = kb.ps("psC_r", [128, 512], F32, p2)
    for ts in range(16):
        t0 = ts * 128

        def rmms(t, t0=t0):
            ins = None
            for kc in range(16):
                ins = t.matmul(ps_r[:, 0:8],
                               xn_own[:, kc * T_OWN + t0:kc * T_OWN + t0 + 128],
                               rwb[:, kc * 8:(kc + 1) * 8],
                               start=(kc == 0), stop=(kc == 15))
            return ins
        TM = sp.T(rmms, extra=[(chE, chE.n)])

        sp.V(lambda v: v.tensor_copy(out=r8[:, 0:8], in_=ps_r[:, 0:8]),
             extra=[(chT, TM)])
        sp.V(lambda v: v.reduce_max(out=r8[:, 8:9], in_=r8[:, 0:8], axis=AX.X))
        sp.V(lambda v: v.tensor_scalar(out=r8[:, 8:9], in0=r8[:, 8:9], scalar1=-1.0,
                                       scalar2=None, op0=ALU.mult))

        def rexp(s, ts=ts):
            return s.activation(out=r8[:, 16:24], in_=r8[:, 0:8], func=AF.Exp,
                                bias=r8[:, 8:9])
        sp.S(rexp)

        sp.V(lambda v: v.reduce_sum(out=r8[:, 9:10], in_=r8[:, 16:24], axis=AX.X))
        sp.V(lambda v: v.reciprocal(out=r8[:, 10:11], in_=r8[:, 9:10]))
        sp.V(lambda v: v.tensor_scalar(out=r8[:, 16:24], in0=r8[:, 16:24],
                                       scalar1=r8[:, 10:11], scalar2=None,
                                       op0=ALU.mult))
        sp.V(lambda v: v.max(out=r8[:, 24:32], in_=r8[:, 16:24]))
        sp.V(lambda v: v.tensor_scalar(out=r8[:, 32:40], in0=r8[:, 16:24],
                                       scalar1=r8[:, 25:26], scalar2=None,
                                       op0=ALU.is_ge))
        EV = sp.V(lambda v: v.tensor_tensor(out=r8[:, 40:48], in0=r8[:, 16:24],
                                            in1=r8[:, 32:40], op=ALU.mult))

        for base in (0, 4):
            def crows(t, base=base):
                ins = None
                for j in range(4):
                    ins = t.matmul(ps_r[0:1, j * 128:(j + 1) * 128],
                                   r8[:, 40 + base + j:41 + base + j],
                                   cm['identf'][:, :], start=True, stop=True)
                return ins
            TM = sp.T(crows, extra=[(chE, EV)] if base == 0 else [(chE, chE.n)])

            STP = s_st.n
            def crevac(s, base=base):
                return s.activation(out=crowt[0:1, :], in_=ps_r[0:1, 0:512],
                                    func=AF.Identity)
            EV2 = sp.S(crevac, extra=[(chT, TM), (s_st, STP)])
            sp.dma(kb.sy, s_st, lambda sync, base=base, ts=ts: [sync.dma_start(
                out=combscr[ts:ts + 1, base * 128:base * 128 + 512],
                in_=crowt[0:1, :])],
                extra=[(chE, EV2)])
    C2_ST = s_st.n
    p2.close()

    p3 = ExitStack()
    hbuf = kb.sb("hbuf", [128, 11 * 2048], BF16, p3)
    sig = kb.sb("sig", [128, 2048], F32, p3)
    wb2 = [kb.sb(f"wb2{j}", [128, 16 * 128], BF16, p3) for j in range(2)]
    wb3 = [kb.sb(f"wb3{j}", [128, 16 * 128], BF16, p3) for j in range(2)]
    wbd2 = [kb.sb(f"wbd2{j}", [128, 11 * 128], BF16, p3) for j in range(2)]
    cbe = kb.sb("cbe", [128, 2048], BF16, p3)
    crow2 = kb.sb("crow2", [1, 2048], F32, p3)
    de = [kb.sb(f"de{j}", [128, 2048], F32, p3) for j in range(2)]
    ps_ug = kb.ps("psC_ug", [128, 4096], F32, p3)
    ps_dn = [ps_ug[:, 0:2048], ps_ug[:, 2048:4096]]

    ACCM = {}
    ACC_ALL = [0]

    def expert_pass(ei):
        shared = ei < 0
        wg_src = shgT if shared else wgTs[ei * D:(ei + 1) * D, :]
        wu_src = shuT if shared else wuTs[ei * D:(ei + 1) * D, :]
        wd_src = shdT if shared else wdTs[ei * 1408:(ei + 1) * 1408, :]
        wg_r = wg_src.rearrange("(kc p) m -> p kc m", p=128)
        wu_r = wu_src.rearrange("(kc p) m -> p kc m", p=128)
        wd_r = wd_src.rearrange("(kc p) m -> p kc m", p=128)

        if not shared:
            CRL = sp.dma(kb.sy, s_lds, lambda sync, ei=ei: [sync.dma_start(
                out=crow2[:, :].rearrange("o (a t) -> o a t", a=16),
                in_=combscr[:, ei * 128:(ei + 1) * 128].unsqueeze(0))],
                extra=[(s_st, C2_ST), (chT, chT.n)])

            def cbmm(t, ei=ei):
                ins = None
                for j in range(4):
                    ins = t.matmul(ps_dn[0][:, j * 512:(j + 1) * 512],
                                   cm['onesrow'][0:1, :],
                                   crow2[0:1, j * 512:(j + 1) * 512],
                                   start=True, stop=True)
                return ins
            TM = sp.T(cbmm, extra=[(s_lds, CRL), (chE, chE.n)])
            sp.S(lambda s: s.activation(out=cbe[:, :], in_=ps_dn[0][:, :],
                                        func=AF.Identity), extra=[(chT, TM)])

        for oc in range(11):
            gi = oc % 2
            WL = sp.dma(kb.g, s_ldg, lambda g, oc=oc, gi=gi: [
                g.dma_start(out=wb2[gi][:, :].rearrange("p (kc m) -> p kc m", kc=16),
                            in_=wg_r[:, :, oc * 128:(oc + 1) * 128]),
                g.dma_start(out=wb3[gi][:, :].rearrange("p (kc m) -> p kc m", kc=16),
                            in_=wu_r[:, :, oc * 128:(oc + 1) * 128])],
                extra=[(chT, chT.n), (chE, chE.n)] if oc < 2 else
                      [(chT, TMm[oc - 2])])

            def ugmms(t, oc=oc, gi=gi):
                ins = None
                for kc in range(16):
                    for tsub in range(4):
                        a = xn_own[:, kc * T_OWN + tsub * 512:kc * T_OWN + tsub * 512 + 512]
                        t.matmul(ps_ug[:, tsub * 512:(tsub + 1) * 512],
                                 wb2[gi][:, kc * 128:(kc + 1) * 128], a,
                                 start=(kc == 0), stop=(kc == 15))
                        ins = t.matmul(ps_ug[:, 2048 + tsub * 512:2048 + tsub * 512 + 512],
                                       wb3[gi][:, kc * 128:(kc + 1) * 128], a,
                                       start=(kc == 0), stop=(kc == 15))
                return ins
            ext = [(s_ldg, WL), (chE, chE.n)]
            TMm[oc] = sp.T(ugmms, extra=ext)

            def ugsig(s, oc=oc):
                return s.activation(out=sig[:, :], in_=ps_ug[:, 0:2048],
                                    func=AF.Sigmoid)
            sp.S(ugsig, extra=[(chT, TMm[oc])])

            def ugh(v, oc=oc, shared=shared):
                ins = v.tensor_tensor(out=hbuf[:, oc * 2048:(oc + 1) * 2048],
                                      in0=ps_ug[:, 2048:4096], in1=sig[:, :],
                                      op=ALU.mult)
                if not shared:
                    ins = v.tensor_tensor(out=hbuf[:, oc * 2048:(oc + 1) * 2048],
                                          in0=hbuf[:, oc * 2048:(oc + 1) * 2048],
                                          in1=cbe[:, :], op=ALU.mult)
                return ins
            sp.V(ugh)

        for oc in range(16):
            gi = oc % 2
            WL = sp.dma(kb.g, s_ldg, lambda g, oc=oc, gi=gi: [
                g.dma_start(out=wbd2[gi][:, :].rearrange("p (kc m) -> p kc m", kc=11),
                            in_=wd_r[:, :, oc * 128:(oc + 1) * 128])],
                extra=[(chT, TMd[oc - 2]) if oc >= 2 else (chE, chE.n)])

            def dnmms(t, oc=oc, gi=gi):
                ins = None
                for kc in range(11):
                    for tsub in range(4):
                        ins = t.matmul(ps_dn[gi][:, tsub * 512:(tsub + 1) * 512],
                                       wbd2[gi][:, kc * 128:(kc + 1) * 128],
                                       hbuf[:, kc * 2048 + tsub * 512:kc * 2048 + tsub * 512 + 512],
                                       start=(kc == 0), stop=(kc == 10))
                return ins
            ext = [(s_ldg, WL), (chE, chE.n)]
            TMd[oc] = sp.T(dnmms, extra=ext)

            def dnev(s, oc=oc, gi=gi):
                return s.activation(out=de[gi][:, :], in_=ps_dn[gi][:, :],
                                    func=AF.Identity)
            EV = sp.S(dnev, extra=[(chT, TMd[oc]), (s_acc, ACCM.get((ei, oc - 2), 0))])

            if shared:
                ACCM[(ei, oc)] = sp.dma(kb.g, s_acc,
                                        lambda g, oc=oc, gi=gi: [g.dma_start(
                    out=acc_r[:, oc, :], in_=de[gi][:, :])], extra=[(chE, EV)])
            else:
                prev = ACCM.get((ei - 1, oc), ACC_ALL[0])
                ACCM[(ei, oc)] = sp.dma(kb.g, s_acc,
                                        lambda g, oc=oc, gi=gi: [g.dma_start(
                    out=acc_r[:, oc, :], in_=de[gi][:, :], accum_op=ALU.add)],
                    extra=[(chE, EV), (s_acc, prev)])

    TMm, TMd = {}, {}
    expert_pass(-1)
    ACC_ALL[0] = s_acc.n
    for ei in range(8):
        TMm, TMd = {}, {}
        expert_pass(ei)
    C4_ACC = s_acc.n
    p3.close()

    p5 = ExitStack()
    ac_t = [kb.sb(f"ac{j}", [128, 2048], F32, p5) for j in range(2)]
    sa_t = [kb.sb(f"sa{j}", [128, 2048], F32, p5) for j in range(2)]
    sb_t = [kb.sb(f"sb{j}", [128, 2048], F32, p5) for j in range(2)]
    xx_t = [kb.sb(f"xx{j}", [128, 2048], F32, p5) for j in range(2)]
    g1_t = kb.sb("g1c", [128, 2048], F32, p5)
    o_t = [kb.sb(f"oc{j}", [128, 2048], F32, p5) for j in range(2)]
    EV5 = {}
    st5 = {}
    for oc in range(16):
        gi = oc % 2
        LD = sp.dma(kb.sy, s_lds, lambda sync, oc=oc, gi=gi: [
            sync.dma_start(out=ac_t[gi][:, :], in_=acc_r[:, oc, :]),
            sync.dma_start(out=sa_t[gi][:, :], in_=s1a_r[:, oc, :]),
            sync.dma_start(out=sb_t[gi][:, :], in_=s1b_r[:, oc, :]),
            sync.dma_start(out=xx_t[gi][:, :], in_=x1_r[:, oc, :])],
            extra=[(s_acc, C4_ACC)] + ([(chE, EV5[oc - 2])] if oc >= 2 else []))

        def fadd(v, oc=oc, gi=gi):
            return v.tensor_tensor(out=ac_t[gi][:, :], in0=ac_t[gi][:, :],
                                   in1=xn_own[:, oc * T_OWN:(oc + 1) * T_OWN],
                                   op=ALU.add)
        sp.V(fadd, extra=[(s_lds, LD)])

        def fsil(s, oc=oc, gi=gi):
            return s.activation(out=g1_t[:, :], in_=ac_t[gi][:, :], func=AF.Silu)
        sp.S(fsil, extra=[(s_st, st5.get(oc - 2, 0))])

        def fcomb(v, oc=oc, gi=gi):
            v.tensor_tensor(out=o_t[gi][:, :], in0=sa_t[gi][:, :], in1=sb_t[gi][:, :],
                            op=ALU.add)
            v.tensor_tensor(out=o_t[gi][:, :], in0=o_t[gi][:, :], in1=g1_t[:, :],
                            op=ALU.mult)
            return v.tensor_tensor(out=o_t[gi][:, :], in0=o_t[gi][:, :],
                                   in1=xx_t[gi][:, :], op=ALU.add)
        EV5[oc] = sp.V(fcomb)
        sp.dma(kb.sy, s_st, lambda sync, oc=oc, gi=gi: [sync.dma_start(
            out=x2h_r[:, oc, :], in_=o_t[gi][:, :])], extra=[(chE, EV5[oc])])
        st5[oc] = s_st.n

    TOTAL_ST = s_st.n
    def final_wait(sync):
        sync.wait_ge(s_st.h, TOTAL_ST)
    kb.sy(final_wait)
    return kb


import ml_dtypes
BF = ml_dtypes.bfloat16


def had(n):
    H = np.array([[1.0]], dtype=np.float32)
    while H.shape[0] < n:
        H = np.block([[H, H], [H, -H]])
    return H


def l0_consts():
    c = {}
    c['c_onesb'] = np.ones((128, 1), BF)
    c['c_onesf'] = np.ones((128, 1), np.float32)
    c['c_onesro'] = np.ones((1, 128), np.float32)
    c['c_h128'] = had(128).astype(BF)
    dupc = np.zeros((32, 128), np.float32)
    dups = np.zeros((32, 128), np.float32)
    for j in range(128):
        i = (j % 64) // 2
        dupc[i, j] = 1.0
        dups[i, j] = -1.0 if (j % 2 == 0) else 1.0
    c['c_dupc'] = dupc
    c['c_dups'] = dups
    return c


def l0_shared(ins):
    f32 = lambda a: np.ascontiguousarray(np.asarray(a, np.float32))
    d = {}
    Wq = np.asarray(ins['L0_attn_Wq']).reshape(2048, 16, 192)
    d['wq_n'] = f32(Wq[:, :, :128].reshape(2048, 2048))
    d['wq_r'] = f32(Wq[:, :, 128:].reshape(2048, 1024))
    Wkva = np.asarray(ins['L0_attn_Wkv_a'])
    d['wkva_c'] = f32(Wkva[:, :512])
    d['wkva_r'] = f32(Wkva[:, 512:])
    d['kvg'] = f32(np.asarray(ins['L0_attn_kv_gamma']).reshape(4, 128).T)
    d['kvbt'] = f32(np.asarray(ins['L0_attn_kv_beta']).reshape(4, 128).T)
    Wkvb = np.asarray(ins['L0_attn_Wkv_b']).reshape(512, 16, 256)
    d['wkvb_k'] = f32(Wkvb[:, :, :128].reshape(512, 2048))
    d['wkvb_v'] = f32(Wkvb[:, :, 128:].reshape(512, 2048))
    d['wout'] = f32(ins['L0_attn_Wout'])
    d['freqsT'] = f32(np.asarray(ins['L0_attn_freqs_cis']).T)
    d['su_ug'] = f32(np.asarray(ins['L0_ffn_SU_ug']).reshape(16, 128).T)
    d['su_dp'] = f32(np.asarray(ins['L0_ffn_SU_dp']).reshape(64, 128).T)
    wsc = np.asarray(ins['L0_ffn_Wscale_ug'])
    d['wsc_ug'] = f32(np.hstack([wsc[:8192].reshape(64, 128).T,
                                 wsc[8192:].reshape(64, 128).T]))
    d['wsc_dp'] = f32(np.asarray(ins['L0_ffn_Wscale_dp']).reshape(16, 128).T)
    Wu = np.asarray(ins['L0_ffn_W_up']).T.reshape(2048, 64, 1, 128)
    Wg = np.asarray(ins['L0_ffn_W_gate']).T.reshape(2048, 64, 1, 128)
    d['wug'] = f32(np.concatenate([Wu, Wg], axis=2).reshape(2048, 16384))
    d['wdownT'] = f32(np.asarray(ins['L0_ffn_W_down']).T)
    d.update(l0_consts())
    return d


def l0_core_inputs(ins, shared, core):
    b, hf = core // 2, core % 2
    x = np.asarray(ins['L0_attn_hidden_states'])[b]
    own = x[hf * 2048:(hf + 1) * 2048]
    oth = x[(1 - hf) * 2048:(2 - hf) * 2048]
    xin = np.ascontiguousarray(np.concatenate([own, oth], axis=0).T.astype(np.float32))
    m = dict(shared)
    m['xin'] = xin
    return m


def l1_consts():
    c = {}
    c['c_onesb'] = np.ones((128, 1), BF)
    c['c_onesf'] = np.ones((128, 1), np.float32)
    c['c_onesro'] = np.ones((1, 128), np.float32)
    c['c_identb'] = np.eye(128, dtype=np.float32).astype(BF)
    c['c_identf'] = np.eye(128, dtype=np.float32)
    c['c_ut'] = np.triu(np.ones((128, 128), np.float32))
    m = np.zeros((128, 128), np.float32)
    m[np.tril_indices(128, -1)] = -1e30
    c['c_mask8'] = np.ascontiguousarray(np.tile(m, (1, 8)))
    return c


def l1a_shared(ins):
    f32 = lambda a: np.ascontiguousarray(np.asarray(a, np.float32))
    d = dict(l1_consts())
    del d['c_identb']
    d = dict(l1_consts())
    d['onw'] = f32(np.asarray(ins['L1_attn_o_norm_weight']).reshape(128, 1))
    return d


def l1a_core_inputs(ins, x1_fm_by_batch, shared, core):
    b, hf = core // 2, core % 2
    hs = hf * 8
    f32 = lambda a: np.ascontiguousarray(np.asarray(a, np.float32))
    m = dict(shared)
    m['x1'] = x1_fm_by_batch[b]
    m['wq1'] = f32(np.asarray(ins['L1_attn_Wq'])[:, hs * 128:(hs + 8) * 128])
    m['wk1'] = f32(np.asarray(ins['L1_attn_Wk'])[:, hs * 128:(hs + 8) * 128])
    m['wv1'] = f32(np.asarray(ins['L1_attn_Wv'])[:, hs * 128:(hs + 8) * 128])
    m['wab'] = f32(np.hstack([np.asarray(ins['L1_attn_Wa'])[:, hs:hs + 8],
                              np.asarray(ins['L1_attn_Wb'])[:, hs:hs + 8]]))
    m['bab'] = f32(np.concatenate([np.asarray(ins['L1_attn_ba'])[hs:hs + 8],
                                   np.asarray(ins['L1_attn_bb'])[hs:hs + 8]]).reshape(1, 16))
    m['lamb_p'] = f32(np.asarray(ins['L1_attn_lambda_params'])
                      [hs * 128:(hs + 8) * 128].reshape(8, 128).T)
    m['wo_r'] = f32(np.asarray(ins['L1_attn_Wo'])[hs * 128:(hs + 8) * 128, :])
    return m


def l1c_shared(ins):
    f32 = lambda a: np.ascontiguousarray(np.asarray(a, np.float32))
    d = {}
    c = l1_consts()
    for k in ('c_onesb', 'c_onesf', 'c_onesro', 'c_identb', 'c_identf'):
        d[k] = c[k]
    d['router_wT'] = f32(np.asarray(ins['L1_ffn_router_w']).T)
    d['shgT'] = f32(np.asarray(ins['L1_ffn_shared_gate']).T)
    d['shuT'] = f32(np.asarray(ins['L1_ffn_shared_up']).T)
    d['shdT'] = f32(np.asarray(ins['L1_ffn_shared_down']).T)
    d['wgTs'] = f32(np.asarray(ins['L1_ffn_w_gate']).transpose(0, 2, 1).reshape(16384, 1408))
    d['wuTs'] = f32(np.asarray(ins['L1_ffn_w_up']).transpose(0, 2, 1).reshape(16384, 1408))
    d['wdTs'] = f32(np.asarray(ins['L1_ffn_w_down']).transpose(0, 2, 1).reshape(11264, 2048))
    return d


def l1c_core_inputs(x1_fm_by_batch, s1p_by_core, shared, core):
    b, hf = core // 2, core % 2
    own = slice(hf * 2048, (hf + 1) * 2048)
    m = dict(shared)
    m['x1own'] = np.ascontiguousarray(x1_fm_by_batch[b][:, own])
    m['s1a'] = np.ascontiguousarray(s1p_by_core[2 * b][:, own])
    m['s1b'] = np.ascontiguousarray(s1p_by_core[2 * b + 1][:, own])
    return m


from concourse.bass_utils import run_bass_kernel_spmd


def kernel(**inputs):
    ins = {k: np.asarray(v) for k, v in inputs.items()}
    cores = list(range(8))

    sh0 = l0_shared(ins)
    maps0 = [l0_core_inputs(ins, sh0, c) for c in cores]
    kb0 = build_l0()
    r0 = run_bass_kernel_spmd(kb0.nc, maps0, core_ids=cores).results
    x1h = [np.asarray(r0[c]["x1h"], np.float32) for c in cores]
    x1_fm = [np.ascontiguousarray(np.hstack([x1h[2 * b], x1h[2 * b + 1]]))
             for b in range(4)]

    sh1 = l1a_shared(ins)
    maps1 = [l1a_core_inputs(ins, x1_fm, sh1, c) for c in cores]
    kb1 = build_l1a()
    r1 = run_bass_kernel_spmd(kb1.nc, maps1, core_ids=cores).results
    s1p = {c: np.asarray(r1[c]["s1p"], np.float32) for c in cores}

    shc = l1c_shared(ins)
    mapsc = [l1c_core_inputs(x1_fm, s1p, shc, c) for c in cores]
    kbc = build_l1c()
    rc = run_bass_kernel_spmd(kbc.nc, mapsc, core_ids=cores).results

    out = np.empty((4, 4096, 2048), np.float32)
    for c in cores:
        b, hf = c // 2, c % 2
        out[b, hf * 2048:(hf + 1) * 2048, :] = np.asarray(rc[c]["x2h"], np.float32).T
    return out
